# version 02
# baseline (speedup 1.0000x reference)
import numpy as np
import ml_dtypes
from contextlib import ExitStack

import concourse.bass as bass
import concourse.mybir as mybir
from concourse.bass_utils import run_bass_kernel_spmd

F32 = mybir.dt.float32
BF16 = mybir.dt.bfloat16
I32 = mybir.dt.int32
F16 = mybir.dt.float16
AF = mybir.ActivationFunctionType
ALU = mybir.AluOpType
AX = mybir.AxisListType.X
bfloat16 = ml_dtypes.bfloat16

NCORES = 8
B, S, D = 16, 2048, 2048
BC = B // NCORES
T = BC * S
H, HD = 16, 128
NCH = D // 128
M_MEM = 4
I0 = 8192
I1 = 5632
EPS_LN = 1e-5
RS2048 = float(1.0 / np.sqrt(np.float32(2048.0)))
RS8192 = float(1.0 / np.sqrt(np.float32(8192.0)))


class Op:
    __slots__ = ("fn", "waits", "incs")
    def __init__(self, fn, waits, incs):
        self.fn = fn; self.waits = waits; self.incs = incs


class Ctx:
    ENGS = ("sync", "tensor", "vector", "scalar", "gpsimd")

    def __init__(self, nc):
        self.nc = nc
        self.es = ExitStack()
        self.prog = {e: [] for e in self.ENGS}
        self.semh = {}
        self.semc = {}

    def sem(self, name):
        if name not in self.semh:
            self.semh[name] = self.es.enter_context(self.nc.semaphore(name))
            self.semc[name] = 0
        return self.semh[name]

    def val(self, name):
        self.sem(name)
        return self.semc[name]

    def sbuf(self, name, shape, dtype=F32, **kw):
        return self.es.enter_context(self.nc.sbuf_tensor(name, shape, dtype, **kw))

    def psum(self, name, shape, dtype=F32):
        return self.es.enter_context(self.nc.psum_tensor(name, shape, dtype))

    def op(self, eng, fn, wait=(), inc=(), dma=False):
        waits = [(self.sem(s), v) for (s, v) in wait if v > 0]
        incs = []
        out = {}
        for s in inc:
            h = self.sem(s)
            d = 16 if dma else 1
            self.semc[s] += d
            out[s] = self.semc[s]
            incs.append((h, d))
        self.prog[eng].append(Op(fn, waits, incs))
        return out

    def drain(self, eng):
        self.prog[eng].append(Op(lambda e: e.drain(), [], []))

    def build(self):
        block = self.es.enter_context(self.nc.Block())
        def mk(engname):
            def run(eng):
                for item in self.prog[engname]:
                    for s, v in item.waits:
                        eng.wait_ge(s, v)
                    ins = item.fn(eng)
                    for s, d in item.incs:
                        ins.then_inc(s, d)
            return run
        for e in self.ENGS:
            getattr(block, e)(mk(e))
        self.es.close()
        return self.nc


def dram_in(nc, name, shape, dtype=F32):
    return nc.dram_tensor(name, shape, dtype, kind="ExternalInput").ap()

def dram_out(nc, name, shape, dtype=F32):
    return nc.dram_tensor(name, shape, dtype, kind="ExternalOutput").ap()


def emit_preln(ctx, x_hbm, u_hbm, p, n_tiles=T // 128, width=D):
    xb = [ctx.sbuf(f"{p}x{i}", [128, width], F32) for i in range(2)]
    ub = [ctx.sbuf(f"{p}u{i}", [128, width], BF16) for i in range(2)]
    nq = width // 512
    bn6 = ctx.sbuf(f"{p}bn6", [128, 6 * nq])
    mv = ctx.sbuf(f"{p}mv", [128, 2])
    sd = ctx.sbuf(f"{p}sd", [128, 1])
    rr = ctx.sbuf(f"{p}rr", [128, 1])
    bcol = ctx.sbuf(f"{p}bc", [128, 1])
    epsc = ctx.sbuf(f"{p}ep", [128, 1])
    ld, smv, ssd, srb, act, st = (p + s for s in ("ld", "mv", "sd", "rb", "ac", "st"))
    ctx.op("vector", lambda e: e.memset(epsc[:], EPS_LN))
    ctx.drain("vector")
    for i in range(n_tiles):
        bi = i % 2
        w = [(act, i - 1)] if i >= 2 else []
        lv = ctx.op("sync", lambda e, b=bi, i_=i: e.dma_start(
            out=xb[b][:], in_=x_hbm[i_ * 128:(i_ + 1) * 128, :]),
            wait=w, inc=(ld,), dma=True)[ld]
        for q in range(nq):
            ctx.op("vector", lambda e, b=bi, q_=q: e.bn_stats(
                bn6[:, q_ * 6:(q_ + 1) * 6], xb[b][:, q_ * 512:(q_ + 1) * 512]),
                wait=([(ld, lv)] if q == 0 else ()))
        ctx.drain("vector")
        mvv = ctx.op("vector", lambda e: e.bn_aggr(mv[:], bn6[:]), inc=(smv,))[smv]
        sdv = ctx.op("scalar", lambda e: e.activation(sd[:], mv[:, 1:2], AF.Sqrt, bias=epsc[:]),
               wait=((smv, mvv),), inc=(ssd,))[ssd]
        ctx.op("vector", lambda e: e.reciprocal(rr[:], sd[:]), wait=((ssd, sdv),))
        ctx.drain("vector")
        wst = []
        if i >= 2:
            wst.append((st + str(i % 2), 16 * (i // 2)))
        av = ctx.op("vector", lambda e, b=bi: e.tensor_scalar(
            ub[b][:], xb[b][:], mv[:, 0:1], rr[:], ALU.subtract, ALU.mult),
            wait=wst, inc=(act,))[act]
        ctx.op("gpsimd", lambda e, b=bi, i_=i: e.dma_start(
            out=u_hbm[i_ * 128:(i_ + 1) * 128, :], in_=ub[b][:]),
            wait=((act, av),), inc=(st + str(i % 2),), dma=True)
    return (st + "0", st + "1")


def emit_load_u_fm(ctx, u_hbm, u_res, p, width=D, t=T, pre_wait=()):
    ld = p + "tld"
    for c in range(width // 128):
        ctx.op("sync", lambda e, c_=c: e.dma_start_transpose(
            u_res[c_][:], u_hbm[:, c_ * 128:(c_ + 1) * 128]),
            wait=(pre_wait if c == 0 else ()), inc=(ld,), dma=True)
    return ld, ctx.val(ld)


def emit_wT_group_load(ctx, w_hbm, wbuf, grp, p, queue="sync", e_width=512, n_chunk=NCH):
    ld = p + "wld"
    e0 = grp * e_width
    for c in range(n_chunk):
        ctx.op(queue, lambda e, c_=c: e.dma_start_transpose(
            wbuf[c_][:], w_hbm[e0:e0 + e_width, c_ * 128:(c_ + 1) * 128]),
            inc=(ld,), dma=True)
    return ld, ctx.val(ld)


def build_l0_proj(ctx, io):
    u_hbm, wq, wk, wv = io["u0_tm"], io["wq0b"], io["wk0b"], io["wv0b"]
    sq_fm, sk_fm, v_tm, qbar_o = io["sq_fm"], io["sk_fm"], io["v_tm"], io["qbar"]

    u_res = [ctx.sbuf(f"u{c}", [128, T], BF16) for c in range(NCH)]
    wbuf = [[ctx.sbuf(f"w{i}_{c}", [128, 512], BF16, is_transpose_dest=True)
             for c in range(NCH)] for i in range(2)]
    ps = [ctx.psum(f"pp{i}", [128, 512], F32) for i in range(2)]
    sig = [ctx.sbuf(f"sg{i}", [128, 512], BF16) for i in range(2)]
    tmp = [ctx.sbuf(f"tp{i}", [128, 512], F32) for i in range(2)]
    exp = [ctx.sbuf(f"ex{i}", [128, 512], F32) for i in range(2)]
    qacc = ctx.sbuf("qacc", [128, 2 * H], F32)
    qred = ctx.sbuf("qred", [128, 1], F32)

    uld, uldv = emit_load_u_fm(ctx, u_hbm, u_res, "l0p")
    ctx.op("vector", lambda e: e.memset(qacc[:], 0.0))
    ctx.drain("vector")

    mm, ev1, ea1, ev2, wst = "mmq", "ev1", "ea1", "ev2", "wst"
    gidx = 0
    tile = 0

    def proj_qk(w_hbm, out_fm, is_q):
        nonlocal gidx, tile
        for grp in range(4):
            bi = gidx % 2
            if gidx >= 2:
                ctx.op("sync", lambda e: e.nop(), wait=((mm, 32 * (gidx - 1)),))
            wld, wldv = emit_wT_group_load(ctx, w_hbm, wbuf[bi], grp, f"g{gidx}")
            for tt in range(8):
                et4 = grp * 4
                for es in range(4):
                    e_tile = et4 + es
                    pb = tile % 2
                    tw = [(wld, wldv)]
                    if tile >= 2:
                        tw.append((ev2, tile - 1))
                    if tile == 0:
                        tw.append((uld, uldv))
                    for c in range(NCH):
                        r = ctx.op("tensor", lambda e, c_=c, pb_=pb, bi_=bi, es_=es, tt_=tt: e.matmul(
                            ps[pb_][:], wbuf[bi_][c_][:, es_ * 128:(es_ + 1) * 128],
                            u_res[c_][:, tt_ * 512:(tt_ + 1) * 512],
                            start=(c_ == 0), stop=(c_ == NCH - 1)),
                            wait=(tw if c == 0 else ()),
                            inc=((mm,) if c == NCH - 1 else ()))
                    mmv = r[mm]
                    wv_ = [(mm, mmv)]
                    if is_q:
                        ctx.op("vector", lambda e, pb_=pb: e.reduce_sum(
                            qred[:], ps[pb_][:], axis=AX), wait=wv_)
                        ctx.drain("vector")
                        col = (0 if tt < 4 else H) + e_tile
                        ctx.op("vector", lambda e, c_=col: e.tensor_tensor(
                            qacc[:, c_:c_ + 1], qacc[:, c_:c_ + 1], qred[:], ALU.add))
                        ctx.drain("vector")
                        wv_ = ()
                    v1 = ctx.op("vector", lambda e, pb_=pb: e.tensor_scalar(
                        tmp[pb_][:], ps[pb_][:], 0.0, None, ALU.min), wait=wv_,
                        inc=(ev1,))[ev1]
                    a1 = ctx.op("scalar", lambda e, pb_=pb: e.activation(
                        exp[pb_][:], tmp[pb_][:], AF.Exp), wait=((ev1, v1),),
                        inc=(ea1,))[ea1]
                    stw = [(ea1, a1)]
                    if tile >= 2:
                        stw.append((wst + str(tile % 2), 16 * (tile // 2)))
                    v2 = ctx.op("vector", lambda e, pb_=pb: e.scalar_tensor_tensor(
                        sig[pb_][:], ps[pb_][:], 0.0, exp[pb_][:], ALU.max, ALU.add),
                        wait=stw, inc=(ev2,))[ev2]
                    ctx.op("gpsimd", lambda e, pb_=pb, e_=e_tile, tt_=tt: e.dma_start(
                        out=out_fm[e_ * 128:(e_ + 1) * 128, tt_ * 512:(tt_ + 1) * 512],
                        in_=sig[pb_][:]), wait=((ev2, v2),),
                        inc=(wst + str(tile % 2),), dma=True)
                    tile += 1
            gidx += 1

    proj_qk(wq, sq_fm, True)
    qv = ctx.val(ev2)
    ctx.op("gpsimd", lambda e: e.dma_start(out=qbar_o, in_=qacc[:]),
           wait=((ev2, qv),), inc=("qst",), dma=True)
    proj_qk(wk, sk_fm, False)

    vt = [ctx.sbuf(f"vt{i}", [128, 512], BF16) for i in range(2)]
    for grp in range(4):
        bi = gidx % 2
        if gidx >= 2:
            ctx.op("sync", lambda e: e.nop(), wait=((mm, 32 * (gidx - 1)),))
        wld, wldv = emit_wT_group_load(ctx, wv, wbuf[bi], grp, f"g{gidx}")
        for tt in range(T // 128):
            pb = tile % 2
            tw = [(wld, wldv)]
            if tile >= 2:
                tw.append((ev2, tile - 1))
            for c in range(NCH):
                r = ctx.op("tensor", lambda e, c_=c, pb_=pb, bi_=bi, tt_=tt: e.matmul(
                    ps[pb_][:], u_res[c_][:, tt_ * 128:(tt_ + 1) * 128],
                    wbuf[bi_][c_][:],
                    start=(c_ == 0), stop=(c_ == NCH - 1)),
                    wait=(tw if c == 0 else ()),
                    inc=((mm,) if c == NCH - 1 else ()))
            mmv = r[mm]
            stw = [(mm, mmv)]
            if tile >= 2:
                stw.append((wst + str(tile % 2), 16 * (tile // 2)))
            v2 = ctx.op("vector", lambda e, pb_=pb: e.tensor_copy(vt[pb_][:], ps[pb_][:]),
                   wait=stw, inc=(ev2,))[ev2]
            ctx.op("gpsimd", lambda e, pb_=pb, g_=grp, tt_=tt: e.dma_start(
                out=v_tm[tt_ * 128:(tt_ + 1) * 128, g_ * 512:(g_ + 1) * 512],
                in_=vt[pb_][:]), wait=((ev2, v2),),
                inc=(wst + str(tile % 2),), dma=True)
            tile += 1
        gidx += 1

    ctx.op("sync", lambda e: e.nop(), wait=(
        (wst + "0", ctx.val(wst + "0")), (wst + "1", ctx.val(wst + "1")),
        ("qst", 16)))


def _split_batches(x):
    return [np.ascontiguousarray(x[c * BC:(c + 1) * BC].reshape(T, *x.shape[2:]))
            for c in range(NCORES)]


def launch_stage0(inputs):
    nc = bass.Bass("TRN2", target_bir_lowering=False)
    ctx = Ctx(nc)
    srcs = {}
    outs = {}
    names = [("wq0", "L0_attn_w_q", [D, D]), ("wk0", "L0_attn_w_k", [D, D]),
             ("wv0", "L0_attn_w_v", [D, D]), ("wo0", "L0_attn_w_o", [D, D]),
             ("wq1", "L1_attn_Wq", [D, D]), ("wk1", "L1_attn_Wk", [D, D]),
             ("wv1", "L1_attn_Wv", [D, D]), ("wo1", "L1_attn_Wo", [D, D]),
             ("memb", "L0_attn_memories", [M_MEM * H * HD, HD])]
    for nm, key, shape in names:
        srcs[nm] = dram_in(nc, nm, shape)
        outs[nm + "b"] = dram_out(nc, nm + "b", shape, BF16)
    for nm, key, shape in names:
        ctx.op("gpsimd", lambda e, a=outs[nm + "b"], b=srcs[nm]: e.dma_start(out=a, in_=b),
               inc=("c",), dma=True)
    ctx.op("gpsimd", lambda e: e.nop(), wait=(("c", ctx.val("c")),))
    ctx.build()
    in_map = {nm: inputs[key].reshape(shape) if nm == "memb" else inputs[key]
              for nm, key, shape in names}
    in_map = {}
    for nm, key, shape in names:
        in_map[nm] = np.ascontiguousarray(inputs[key].reshape(shape))
    res = run_bass_kernel_spmd(nc, [in_map] * NCORES, core_ids=list(range(NCORES)))
    return res.results


def launch_l0a(inputs):
    nc = bass.Bass("TRN2", target_bir_lowering=False)
    ctx = Ctx(nc)
    x0 = dram_in(nc, "x0", [T, D])
    u0 = dram_out(nc, "u0_tm", [T, D], BF16)
    s0, s1 = emit_preln(ctx, x0, u0, "a")
    ctx.op("sync", lambda e: e.nop(), wait=((s0, ctx.val(s0)), (s1, ctx.val(s1))))
    ctx.build()
    shards = _split_batches(inputs["L0_attn_hidden_states"])
    res = run_bass_kernel_spmd(nc, [{"x0": s} for s in shards],
                               core_ids=list(range(NCORES)))
    return res.results


def launch_l0b(u0_list, wq0b, wk0b, wv0b):
    nc = bass.Bass("TRN2", target_bir_lowering=False)
    ctx = Ctx(nc)
    io = {
        "u0_tm": dram_in(nc, "u0_tm", [T, D], BF16),
        "wq0b": dram_in(nc, "wq0b", [D, D], BF16),
        "wk0b": dram_in(nc, "wk0b", [D, D], BF16),
        "wv0b": dram_in(nc, "wv0b", [D, D], BF16),
        "sq_fm": dram_out(nc, "sq_fm", [D, T], BF16),
        "sk_fm": dram_out(nc, "sk_fm", [D, T], BF16),
        "v_tm": dram_out(nc, "v_tm", [T, D], BF16),
        "qbar": dram_out(nc, "qbar", [128, 2 * H], F32),
    }
    build_l0_proj(ctx, io)
    ctx.build()
    maps = [{"u0_tm": u0_list[c], "wq0b": wq0b, "wk0b": wk0b, "wv0b": wv0b}
            for c in range(NCORES)]
    res = run_bass_kernel_spmd(nc, maps, core_ids=list(range(NCORES)))
    return res.results


def kernel(**inputs):
    r0 = launch_stage0(inputs)
    s0 = r0[0]
    rh = launch_stage_h(inputs)
    MupT, MgateT, MdnT = rh[0]["Mup_T"], rh[0]["Mgate_T"], rh[0]["Mdn_T"]
    rq = launch_stage_q(inputs)
    Wgu_b, Wdn_b = rq[0]["Wgu_b"], rq[0]["Wdn_b"]

    ra = launch_l0a(inputs)
    u0 = [ra[c]["u0_tm"] for c in range(NCORES)]
    rb = launch_l0b(u0, s0["wq0b"], s0["wk0b"], s0["wv0b"])
    rc = launch_l0c([rb[c]["sq_fm"] for c in range(NCORES)],
                    [rb[c]["sk_fm"] for c in range(NCORES)],
                    [rb[c]["v_tm"] for c in range(NCORES)],
                    [rb[c]["qbar"] for c in range(NCORES)],
                    s0["membb"], inputs)
    x0_shards = _split_batches(inputs["L0_attn_hidden_states"])
    rd = launch_l0d([rc[c]["attnout0_tm"] for c in range(NCORES)], s0["wo0b"],
                    x0_shards)
    re_ = launch_l0e([rd[c]["u1_tm"] for c in range(NCORES)], MupT, MgateT, inputs)
    rf = launch_l0f([re_[c]["h_fm"] for c in range(NCORES)], MdnT,
                    [rd[c]["x1"] for c in range(NCORES)], inputs)
    rg = launch_l1a([rf[c]["u2_tm"] for c in range(NCORES)],
                    s0["wq1b"], s0["wk1b"], s0["wv1b"], inputs)
    rhh = launch_l1b([rg[c]["q_fm"] for c in range(NCORES)],
                     [rg[c]["k_fm"] for c in range(NCORES)],
                     [rg[c]["v_tm"] for c in range(NCORES)])
    ri = launch_l1c([rhh[c]["attnout1_fm"] for c in range(NCORES)], s0["wo1b"],
                    [rf[c]["x2"] for c in range(NCORES)], inputs)
    rj = launch_l1d([ri[c]["u3_tm"] for c in range(NCORES)], Wgu_b)
    rk = launch_l1e([rj[c]["h_fm"] for c in range(NCORES)], Wdn_b,
                    [ri[c]["x3"] for c in range(NCORES)])
    out = np.empty((B, S, D), np.float32)
    for c in range(NCORES):
        out[c * BC:(c + 1) * BC] = np.asarray(rk[c]["x4"], np.float32).reshape(BC, S, D)
    return out




def build_l0_attn(ctx, io):
    sq_fm, sk_fm, v_tm = io["sq_fm"], io["sk_fm"], io["v_tm"]
    memb, mnorms, gate, qbar = io["memb"], io["mnorms"], io["gate"], io["qbar"]
    out_tm = io["attnout0_tm"]
    NCK = S // 128
    NBH = 2 * H

    ia = ctx.sbuf("ia", [128, 128], I32)
    ib = ctx.sbuf("ib", [128, 128], I32)
    eye_b = ctx.sbuf("eye_b", [128, 128], BF16)
    maskT = ctx.sbuf("maskT", [128, 128], F32)
    ones_b = ctx.sbuf("ones_b", [128, 1], BF16)
    ones_row_b = ctx.sbuf("ones_row_b", [1, 128], BF16)
    f32t = ctx.sbuf("f32t", [128, 128], F32)
    ctx.op("gpsimd", lambda e: e.iota(ia[:], [[0, 128]], channel_multiplier=1))
    ctx.op("gpsimd", lambda e: e.iota(ib[:], [[1, 128]], channel_multiplier=0),
           inc=("cst",))
    ctx.op("vector", lambda e: e.tensor_tensor(f32t[:], ia[:], ib[:], ALU.is_equal),
           wait=(("cst", 1),))
    ctx.drain("vector")
    ctx.op("vector", lambda e: e.tensor_copy(eye_b[:], f32t[:]))
    ctx.drain("vector")
    ctx.op("vector", lambda e: e.tensor_tensor(maskT[:], ib[:], ia[:], ALU.is_ge))
    ctx.drain("vector")
    ctx.op("vector", lambda e: e.memset(ones_b[:], 1.0))
    ctx.drain("vector")
    ctx.op("vector", lambda e: e.memset(ones_row_b[:], 1.0), inc=("cst",))
    ctx.drain("vector")

    mn64 = ctx.sbuf("mn64", [64, 128], F32)
    mn64b = ctx.sbuf("mn64b", [64, 128], BF16)
    mnT_b = ctx.sbuf("mnT_b", [128, 64], BF16)
    g_row = ctx.sbuf("g_row", [1, H], F32)
    qb_f = ctx.sbuf("qb_f", [128, NBH], F32)
    qb_b = ctx.sbuf("qb_b", [128, NBH], BF16)
    relr = ctx.sbuf("relr", [1, 4 * NBH], F32)
    wrow = ctx.sbuf("wrow", [1, 4 * NBH], F32)
    wrow_b = ctx.sbuf("wrow_b", [1, 4 * NBH], BF16)
    coefb = ctx.sbuf("coefb", [128, 4 * NBH], F32)
    gco = ctx.sbuf("gco", [128, H], F32)
    onegco = ctx.sbuf("onegco", [128, H], F32)
    mrow = ctx.sbuf("mrow", [1, NBH], F32)
    sumr = ctx.sbuf("sumr", [1, NBH], F32)
    emask = ctx.sbuf("emask", [1, 4], F32)
    emrow = ctx.sbuf("emrow", [1, 64], F32)
    tmpr = ctx.sbuf("tmpr", [1, 4 * NBH], F32)
    gb_row = ctx.sbuf("gb_row", [1, H], BF16)
    setup_es = ExitStack()
    pT = setup_es.enter_context(ctx.nc.psum_tensor("pT", [128, 64], BF16))
    pR = setup_es.enter_context(ctx.nc.psum_tensor("pR", [1, 4 * NBH], F32))
    pB = setup_es.enter_context(ctx.nc.psum_tensor("pB", [128, 4 * NBH], F32))
    pE = setup_es.enter_context(ctx.nc.psum_tensor("pE", [1, 64], F32))
    pG = setup_es.enter_context(ctx.nc.psum_tensor("pG", [128, H], F32))

    ctx.op("sync", lambda e: e.dma_start(out=mn64[:], in_=mnorms), inc=("mld",), dma=True)
    ctx.op("sync", lambda e: e.dma_start(out=g_row[:], in_=gate), inc=("mld",), dma=True)
    ctx.op("sync", lambda e: e.dma_start(out=qb_f[:], in_=qbar), inc=("mld",), dma=True)
    MLD = ctx.val("mld")
    ctx.op("vector", lambda e: e.tensor_copy(mn64b[:], mn64[:]), wait=(("mld", MLD),))
    ctx.drain("vector")
    ctx.op("vector", lambda e: e.tensor_scalar(qb_b[:], qb_f[:], 1.0 / S, None, ALU.mult),
           inc=("qbb",))
    ctx.op("tensor", lambda e: e.transpose(pT[:, 0:64], mn64b[:], eye_b[0:64, 0:64]),
           wait=(("cst", 2), ("qbb", 1)), inc=("mtd",))
    ctx.op("vector", lambda e: e.tensor_copy(mnT_b[:], pT[:, 0:64]),
           wait=(("mtd", 1),), inc=("mnb",))
    ctx.drain("vector")
    ctx.op("tensor", lambda e: e.matmul(pE[:], ones_b[:], mnT_b[:], start=True, stop=True),
           wait=(("mnb", 1),), inc=("met",))
    ctx.op("vector", lambda e: e.tensor_copy(emrow[:], pE[:]), wait=(("met", 1),))
    ctx.drain("vector")
    ctx.op("vector", lambda e: e.reduce_sum(
        emask[:], emrow.ap().rearrange("o (m h) -> o m h", m=4), axis=AX))
    ctx.drain("vector")
    ctx.op("vector", lambda e: e.tensor_scalar(emask[:], emask[:], 1e-6, -1e30,
                                               ALU.is_lt, ALU.mult))
    ctx.drain("vector")
    ctx.op("scalar", lambda e: e.activation(g_row[:], g_row[:], AF.Sigmoid),
           wait=(("mld", MLD),), inc=("gsg",))
    ctx.op("vector", lambda e: e.tensor_copy(gb_row[:], g_row[:]),
           wait=(("gsg", 1),), inc=("gbf",))
    ctx.drain("vector")
    ctx.op("tensor", lambda e: e.matmul(pG[:], ones_row_b[:], gb_row[:],
                                        start=True, stop=True),
           wait=(("gbf", 1), ("cst", 2)), inc=("gbc",))
    ctx.op("vector", lambda e: e.tensor_copy(gco[:], pG[:]), wait=(("gbc", 1),))
    ctx.drain("vector")
    ctx.op("vector", lambda e: e.tensor_scalar(onegco[:], gco[:], -1.0, 1.0,
                                               ALU.mult, ALU.add))
    ctx.drain("vector")
    for bh in range(NBH):
        b_, h_ = divmod(bh, H)
        ctx.op("tensor", lambda e, bh_=bh, hh=h_: e.matmul(
            pR[:, bh_ * 4:(bh_ + 1) * 4], qb_b[:, bh_:bh_ + 1],
            mnT_b.ap()[:, hh::16], start=True, stop=True),
            wait=(("mnb", 1),), inc=(("rel",) if bh == NBH - 1 else ()))
    KREL = float(HD ** -0.5)
    ctx.op("vector", lambda e: e.tensor_scalar(relr[:], pR[:], KREL, None, ALU.mult),
           wait=(("rel", 1),))
    ctx.drain("vector")
    for g4 in range(NBH):
        ctx.op("vector", lambda e, g_=g4: e.tensor_tensor(
            relr[:, g_ * 4:(g_ + 1) * 4], relr[:, g_ * 4:(g_ + 1) * 4],
            emask[:], ALU.add))
    ctx.drain("vector")
    ctx.op("vector", lambda e: e.reduce_max(
        mrow[:], relr.ap().rearrange("o (g m) -> o g m", m=4), axis=AX))
    ctx.drain("vector")
    for m in range(4):
        ctx.op("vector", lambda e, m_=m: e.tensor_copy(
            tmpr.ap().rearrange("o (g m) -> o g m", m=4)[:, :, m_], mrow[:]))
    ctx.drain("vector")
    ctx.op("vector", lambda e: e.tensor_tensor(relr[:], relr[:], tmpr[:], ALU.subtract),
           inc=("rex",))
    ctx.op("scalar", lambda e: e.activation(relr[:], relr[:], AF.Exp),
           wait=(("rex", 1),), inc=("rxx",))
    ctx.op("vector", lambda e: e.reduce_sum(
        sumr[:], relr.ap().rearrange("o (g m) -> o g m", m=4), axis=AX),
        wait=(("rxx", 1),))
    ctx.drain("vector")
    ctx.op("vector", lambda e: e.reciprocal(sumr[:], sumr[:]))
    ctx.drain("vector")
    for m in range(4):
        ctx.op("vector", lambda e, m_=m: e.tensor_copy(
            tmpr.ap().rearrange("o (g m) -> o g m", m=4)[:, :, m_], sumr[:]))
    ctx.drain("vector")
    ctx.op("vector", lambda e: e.tensor_tensor(wrow[:], relr[:], tmpr[:], ALU.mult))
    ctx.drain("vector")
    for b_ in range(2):
        for m in range(4):
            ctx.op("vector", lambda e, b2=b_, m_=m: e.tensor_copy(
                tmpr.ap().rearrange("o (b h m) -> o b h m", b=2, m=4)[:, b2, :, m_],
                g_row[:]))
    ctx.drain("vector")
    ctx.op("vector", lambda e: e.tensor_tensor(wrow[:], wrow[:], tmpr[:], ALU.mult))
    ctx.drain("vector")
    ctx.op("vector", lambda e: e.tensor_copy(wrow_b[:], wrow[:]), inc=("wrb",))
    ctx.drain("vector")
    ctx.op("tensor", lambda e: e.matmul(pB[:], ones_row_b[:], wrow_b[:],
                                        start=True, stop=True),
           wait=(("wrb", 1),), inc=("cbc",))
    ctx.op("vector", lambda e: e.tensor_copy(coefb[:], pB[:]), wait=(("cbc", 1),),
           inc=("cbv",))
    ctx.drain("vector")
    setup_es.close()
    if "dbg_coef" in io:
        ctx.op("gpsimd", lambda e: e.dma_start(out=io["dbg_coef"], in_=coefb[:]),
               wait=(("cbv", 1),), inc=("dbg",), dma=True)
        ctx.op("gpsimd", lambda e: e.dma_start(out=io["dbg_wrow"], in_=wrow[:]),
               inc=("dbg",), dma=True)
        ctx.op("gpsimd", lambda e: e.dma_start(out=io["dbg_emask"], in_=emask[:]),
               inc=("dbg",), dma=True)
        ctx.op("gpsimd", lambda e: e.dma_start(out=io["dbg_gco"], in_=gco[:]),
               inc=("dbg",), dma=True)
        ctx.op("gpsimd", lambda e: e.dma_start(out=io["dbg_mnt"], in_=mnT_b[:]),
               inc=("dbg",), dma=True)

    memr = ctx.sbuf("memr", [128, 64 * 128], BF16)
    for mh in range(64):
        ctx.op("sync", lambda e, mh_=mh: e.dma_start(
            out=memr[:, mh_ * 128:(mh_ + 1) * 128],
            in_=memb[mh_ * 128:(mh_ + 1) * 128, :]), inc=("memld",), dma=True)
    MEMLD = ctx.val("memld")

    sqb = [ctx.sbuf(f"sqb{i}", [128, S], BF16) for i in range(2)]
    skb = [ctx.sbuf(f"skb{i}", [128, S], BF16) for i in range(2)]
    vbb = [ctx.sbuf(f"vbb{i}", [128, NCK * 128], BF16) for i in range(2)]
    mems = ctx.sbuf("mems", [128, 512], BF16)
    skT = ctx.sbuf("skT", [128, 128], BF16)
    sTm = ctx.sbuf("sTm", [128, 128], BF16)
    S_f = ctx.sbuf("S_f", [128, 128], F32)
    S_b = ctx.sbuf("S_b", [128, 128], BF16)
    Z_f = ctx.sbuf("Z_f", [128, 1], F32)
    Z_b = ctx.sbuf("Z_b", [128, 1], BF16)
    zc = ctx.sbuf("zc", [128, 1], F32)
    rD = ctx.sbuf("rD", [128, 4], F32)
    rL = ctx.sbuf("rL", [128, 1], F32)
    acc = ctx.sbuf("acc", [128, 128], F32)
    otile = [ctx.sbuf(f"ot{i}", [128, 128], BF16) for i in range(2)]
    pA = ctx.psum("pA", [128, 512], F32)
    pD = ctx.psum("pD", [128, 4], F32)
    pS = ctx.psum("pS", [128, 128], F32)
    pN = ctx.psum("pN", [128, 128], F32)
    pDen = ctx.psum("pDen", [128, 1], F32)
    pTr = ctx.psum("pTr", [128, 128], BF16)
    pK = ctx.psum("pK", [128, 128], F32)

    for bh in range(NBH):
        b_, h_ = divmod(bh, H)
        li = bh % 2
        lw = [("mmK", NCK * (bh - 1)), ("stu", NCK * (bh - 1))] if bh >= 2 else []
        lv = ctx.op("sync", lambda e, li_=li, b2=b_, hh=h_: e.dma_start(
            out=sqb[li_][:], in_=sq_fm[hh * 128:(hh + 1) * 128, b2 * S:(b2 + 1) * S]),
            wait=lw, inc=("sql" + str(li),), dma=True)["sql" + str(li)]
        ctx.op("sync", lambda e, li_=li, b2=b_, hh=h_: e.dma_start(
            out=skb[li_][:], in_=sk_fm[hh * 128:(hh + 1) * 128, b2 * S:(b2 + 1) * S]),
            inc=("sql" + str(li),), dma=True)
        ctx.op("sync", lambda e, li_=li, b2=b_, hh=h_: e.dma_start(
            out=vbb[li_].ap().rearrange("p (c e) -> p c e", e=128),
            in_=v_tm[b2 * S:(b2 + 1) * S, hh * 128:(hh + 1) * 128]
                .rearrange("(c p) e -> p c e", p=128)),
            inc=("sql" + str(li),), dma=True)
        LV = ctx.val("sql" + str(li))
        scw = [("sql" + str(li), LV), ("cbv", 1), ("memld", MEMLD)]
        if bh >= 1:
            scw.append(("mmA", NCK * bh))
        for m in range(4):
            mh = m * H + h_
            col = (b_ * H + h_) * 4 + m
            ctx.op("vector", lambda e, mh_=mh, c_=col, m_=m: e.tensor_scalar(
                mems[:, m_ * 128:(m_ + 1) * 128],
                memr[:, mh_ * 128:(mh_ + 1) * 128],
                coefb[:, c_:c_ + 1], None, ALU.mult),
                wait=(scw if m == 0 else ()))
        ctx.drain("vector")
        mw = [("mmN", NCK * bh), ("mmE", NCK * bh)] if bh >= 1 else []
        ctx.op("vector", lambda e: e.memset(S_f[:], 0.0), wait=mw)
        ctx.op("vector", lambda e: e.memset(S_b[:], 0.0))
        ctx.op("vector", lambda e: e.memset(Z_f[:], 0.0))
        ctx.op("vector", lambda e: e.memset(Z_b[:], 0.0), inc=("memc",))
        ctx.drain("vector")
        MEMC = ctx.val("memc")

        for c in range(NCK):
            cw = [("memc", MEMC)] if c == 0 else []
            if c > 0:
                cw.append(("odn", ctx.val("odn")))
            ctx.op("tensor", lambda e, li_=li, c_=c: e.matmul(
                pS[:], skb[li_][:, c_ * 128:(c_ + 1) * 128],
                sqb[li_][:, c_ * 128:(c_ + 1) * 128], start=True, stop=True),
                wait=cw, inc=("mmS",))
            tw_ = [("vT", ctx.val("vT"))] if ctx.val("vT") > 0 else []
            ctx.op("tensor", lambda e, li_=li, c_=c: e.transpose(
                pTr[:], skb[li_].ap()[:, c_ * 128:(c_ + 1) * 128], eye_b[:]),
                wait=tw_, inc=("mmT",))
            ctx.op("tensor", lambda e, li_=li, c_=c: e.matmul(
                pA[:], sqb[li_][:, c_ * 128:(c_ + 1) * 128], mems[:],
                start=True, stop=True), inc=("mmA",))
            ctx.op("tensor", lambda e, li_=li, c_=c, hh=h_: e.matmul(
                pD[:], sqb[li_][:, c_ * 128:(c_ + 1) * 128],
                mnT_b.ap()[:, hh::16], start=True, stop=True), inc=("mmD",))
            ctx.op("vector", lambda e: e.tensor_tensor(sTm[:], pS[:], maskT[:],
                                                       ALU.mult),
                   wait=(("mmS", ctx.val("mmS")),), inc=("vS",))
            ctx.op("vector", lambda e: e.tensor_copy(skT[:], pTr[:]),
                   wait=(("mmT", ctx.val("mmT")),), inc=("vT",))
            nw = [("vS", ctx.val("vS"))]
            if c > 0:
                ctx.op("tensor", lambda e, li_=li, c_=c: e.matmul(
                    pN[:], sqb[li_][:, c_ * 128:(c_ + 1) * 128], S_b[:],
                    start=True, stop=False), wait=[("stu", ctx.val("stu"))] + nw)
                ctx.op("tensor", lambda e, li_=li, c_=c: e.matmul(
                    pN[:], sTm[:], vbb[li_][:, c_ * 128:(c_ + 1) * 128],
                    start=False, stop=True), inc=("mmN",))
            else:
                ctx.op("tensor", lambda e, li_=li, c_=c: e.matmul(
                    pN[:], sTm[:], vbb[li_][:, c_ * 128:(c_ + 1) * 128],
                    start=True, stop=True), wait=nw, inc=("mmN",))
            if c > 0:
                ctx.op("tensor", lambda e, li_=li, c_=c: e.matmul(
                    pDen[:], sqb[li_][:, c_ * 128:(c_ + 1) * 128], Z_b[:],
                    start=True, stop=False))
                ctx.op("tensor", lambda e: e.matmul(
                    pDen[:], sTm[:], ones_b[:], start=False, stop=True),
                    inc=("mmE",))
            else:
                ctx.op("tensor", lambda e: e.matmul(
                    pDen[:], sTm[:], ones_b[:], start=True, stop=True),
                    inc=("mmE",))
            ctx.op("tensor", lambda e, li_=li, c_=c: e.matmul(
                pK[:], skT[:], vbb[li_][:, c_ * 128:(c_ + 1) * 128],
                start=True, stop=True), wait=(("vT", ctx.val("vT")),), inc=("mmK",))
            ctx.op("vector", lambda e: e.tensor_tensor(S_f[:], pK[:], S_f[:], ALU.add),
                   wait=(("mmK", ctx.val("mmK")),))
            ctx.drain("vector")
            ctx.op("vector", lambda e: e.tensor_copy(S_b[:], S_f[:]))
            ctx.op("vector", lambda e, li_=li, c_=c: e.reduce_sum(
                zc[:], skb[li_][:, c_ * 128:(c_ + 1) * 128], axis=AX))
            ctx.drain("vector")
            ctx.op("vector", lambda e: e.tensor_tensor(Z_f[:], Z_f[:], zc[:], ALU.add))
            ctx.drain("vector")
            ctx.op("vector", lambda e: e.tensor_copy(Z_b[:], Z_f[:]), inc=("stu",))
            ctx.drain("vector")
            ctx.op("vector", lambda e: e.tensor_scalar(rD[:], pD[:], 1e-6, None,
                                                       ALU.max),
                   wait=(("mmD", ctx.val("mmD")),))
            ctx.drain("vector")
            ctx.op("vector", lambda e: e.reciprocal(rD[:], rD[:]))
            ctx.drain("vector")
            ctx.op("vector", lambda e: e.tensor_scalar(rL[:], pDen[:], 1e-6, None,
                                                       ALU.max),
                   wait=(("mmE", ctx.val("mmE")),))
            ctx.drain("vector")
            ctx.op("vector", lambda e: e.reciprocal(rL[:], rL[:]))
            ctx.drain("vector")
            ctx.op("vector", lambda e, hh=h_: e.tensor_scalar(
                rL[:], rL[:], onegco[:, hh:hh + 1], None, ALU.mult))
            ctx.drain("vector")
            ctx.op("vector", lambda e: e.tensor_scalar(
                acc[:], pA[:, 0:128], rD[:, 0:1], None, ALU.mult),
                wait=(("mmA", ctx.val("mmA")),))
            ctx.drain("vector")
            for m in range(1, 4):
                ctx.op("vector", lambda e, m_=m: e.scalar_tensor_tensor(
                    acc[:], pA[:, m_ * 128:(m_ + 1) * 128], rD[:, m_:m_ + 1],
                    acc[:], ALU.mult, ALU.add))
                ctx.drain("vector")
            ow = [("mmN", ctx.val("mmN"))]
            oi = bh * NCK + c
            if oi >= 2:
                ow.append(("ost" + str(oi % 2), 16 * (oi // 2)))
            ctx.op("vector", lambda e, oi_=oi: e.scalar_tensor_tensor(
                otile[oi_ % 2][:], pN[:], rL[:], acc[:], ALU.mult, ALU.add),
                wait=ow, inc=("odn",))
            ctx.op("gpsimd", lambda e, b2=b_, hh=h_, c_=c, oi_=oi: e.dma_start(
                out=out_tm[b2 * S + c_ * 128: b2 * S + (c_ + 1) * 128,
                           hh * 128:(hh + 1) * 128],
                in_=otile[oi_ % 2][:]),
                wait=(("odn", ctx.val("odn")),), inc=("ost" + str(oi % 2),), dma=True)
            if bh == 0 and c == 0 and "dbg_rd" in io:
                ctx.op("gpsimd", lambda e: e.dma_start(out=io["dbg_rd"], in_=rD[:]),
                       wait=(("odn", ctx.val("odn")),), inc=("dbg",), dma=True)
                ctx.op("gpsimd", lambda e: e.dma_start(out=io["dbg_rl"], in_=rL[:]),
                       inc=("dbg",), dma=True)
                ctx.op("gpsimd", lambda e: e.dma_start(out=io["dbg_acc"], in_=acc[:]),
                       inc=("dbg",), dma=True)
                ctx.op("gpsimd", lambda e: e.dma_start(out=io["dbg_stm"], in_=sTm[:]),
                       inc=("dbg",), dma=True)
                ctx.op("gpsimd", lambda e: e.dma_start(out=io["dbg_sf"], in_=S_f[:]),
                       inc=("dbg",), dma=True)
                ctx.op("gpsimd", lambda e: e.dma_start(out=io["dbg_mems"], in_=mems[:]),
                       inc=("dbg",), dma=True)

    ctx.op("sync", lambda e: e.nop(), wait=(
        ("ost0", ctx.val("ost0")), ("ost1", ctx.val("ost1"))))


def launch_l0c(sq_list, sk_list, v_list, qbar_list, memb, inputs):
    nc = bass.Bass("TRN2", target_bir_lowering=False)
    ctx = Ctx(nc)
    io = {
        "sq_fm": dram_in(nc, "sq_fm", [D, T], BF16),
        "sk_fm": dram_in(nc, "sk_fm", [D, T], BF16),
        "v_tm": dram_in(nc, "v_tm", [T, D], BF16),
        "memb": dram_in(nc, "memb", [M_MEM * H * HD, HD], BF16),
        "mnorms": dram_in(nc, "mnorms", [M_MEM * H, HD], F32),
        "gate": dram_in(nc, "gate", [1, H], F32),
        "qbar": dram_in(nc, "qbar", [128, 2 * H], F32),
        "attnout0_tm": dram_out(nc, "attnout0_tm", [T, D], BF16),
        "dbg_coef": dram_out(nc, "dbg_coef", [128, 128], F32),
        "dbg_wrow": dram_out(nc, "dbg_wrow", [1, 128], F32),
        "dbg_emask": dram_out(nc, "dbg_emask", [1, 4], F32),
        "dbg_gco": dram_out(nc, "dbg_gco", [128, H], F32),
        "dbg_mnt": dram_out(nc, "dbg_mnt", [128, 64], BF16),
        "dbg_rd": dram_out(nc, "dbg_rd", [128, 4], F32),
        "dbg_rl": dram_out(nc, "dbg_rl", [128, 1], F32),
        "dbg_acc": dram_out(nc, "dbg_acc", [128, 128], F32),
        "dbg_stm": dram_out(nc, "dbg_stm", [128, 128], BF16),
        "dbg_sf": dram_out(nc, "dbg_sf", [128, 128], F32),
        "dbg_mems": dram_out(nc, "dbg_mems", [128, 512], BF16),
    }
    build_l0_attn(ctx, io)
    ctx.build()
    mn = np.ascontiguousarray(inputs["L0_attn_memory_norms"].reshape(M_MEM * H, HD))
    gt = np.ascontiguousarray(inputs["L0_attn_gate"].reshape(1, H))
    maps = [{"sq_fm": sq_list[c], "sk_fm": sk_list[c], "v_tm": v_list[c],
             "memb": memb, "mnorms": mn, "gate": gt, "qbar": qbar_list[c]}
            for c in range(NCORES)]
    res = run_bass_kernel_spmd(nc, maps, core_ids=list(range(NCORES)))
    return res.results


def emit_tm_to_fm(ctx, src_tm, dst_fm, p, width=D, t=T):
    buf = [ctx.sbuf(f"{p}b{i}", [128, t], BF16, is_transpose_dest=True) for i in range(2)]
    ld, st = p + "ld", p + "st"
    for c in range(width // 128):
        bi = c % 2
        w = [(st + str(bi), 16 * (c // 2))] if c >= 2 else []
        lv = ctx.op("sync", lambda e, c_=c, b=bi: e.dma_start_transpose(
            buf[b][:], src_tm[:, c_ * 128:(c_ + 1) * 128]),
            wait=w, inc=(ld + str(bi),), dma=True)[ld + str(bi)]
        ctx.op("gpsimd", lambda e, c_=c, b=bi: e.dma_start(
            out=dst_fm[c_ * 128:(c_ + 1) * 128, :], in_=buf[b][:]),
            wait=((ld + str(bi), lv),), inc=(st + str(bi),), dma=True)
    return (st + "0", ctx.val(st + "0")), (st + "1", ctx.val(st + "1"))


def emit_out_proj(ctx, io, p, lhs_fm, w_loader, y_hbm, K_all, out_w=D,
                  bias_bc=None, scale_bc=None, pre_wait=()):
    KCH = K_all // 128
    n_grp = out_w // 512
    nwb = 2 if KCH <= 16 else 1
    wbuf = [[ctx.sbuf(f"{p}w{i}_{c}", [128, 512], BF16, is_transpose_dest=True)
             for c in range(KCH)] for i in range(nwb)]
    abuf = [ctx.sbuf(f"{p}a{i}", [128, KCH * 128], BF16) for i in range(2)]
    ps = [ctx.psum(f"{p}p{i}", [128, 512], F32) for i in range(2)]
    yb = [ctx.sbuf(f"{p}y{i}", [128, 512], F32) for i in range(2)]
    mm, ev = p + "mm", p + "ev"
    tile = 0
    NT = T // 128
    if pre_wait:
        ctx.op("sync", lambda e: e.nop(), wait=pre_wait)
    for g in range(n_grp):
        bi = g % nwb
        if g >= nwb:
            ctx.op("sync", lambda e: e.nop(), wait=((mm, (g - nwb + 1) * NT),))
        wld, wldv = w_loader(ctx, g, wbuf[bi], f"{p}wl{g}")
        for tt in range(NT):
            ai = tile % 2
            alw = [(mm, tile - 1)] if tile >= 2 else []
            alv = ctx.op("sync", lambda e, tt_=tt, ai_=ai: e.dma_start(
                out=abuf[ai_].ap().rearrange("q (c t) -> q c t", t=128),
                in_=lhs_fm[:, tt_ * 128:(tt_ + 1) * 128]
                    .rearrange("(c q) t -> q c t", q=128)),
                wait=alw, inc=(p + "al" + str(ai),), dma=True)[p + "al" + str(ai)]
            tw = [(wld, wldv), (p + "al" + str(ai), alv)]
            if tile >= 2:
                tw.append((ev, tile - 1))
            for c in range(KCH):
                r = ctx.op("tensor", lambda e, c_=c, ai_=ai, bi_=bi, pb=tile % 2: e.matmul(
                    ps[pb][:], abuf[ai_][:, c_ * 128:(c_ + 1) * 128], wbuf[bi_][c_][:],
                    start=(c_ == 0), stop=(c_ == KCH - 1)),
                    wait=(tw if c == 0 else ()),
                    inc=((mm,) if c == KCH - 1 else ()))
            mmv = r[mm]
            stw = [(mm, mmv)]
            if tile >= 2:
                stw.append((p + "st" + str(tile % 2), 16 * (tile // 2)))
            if scale_bc is not None:
                ev_ = ctx.op("vector", lambda e, pb=tile % 2, g_=g: e.tensor_tensor(
                    yb[pb][:], ps[pb][:], scale_bc[:, g_ * 512:(g_ + 1) * 512],
                    ALU.mult), wait=stw, inc=(ev,))[ev]
            elif bias_bc is not None:
                ev_ = ctx.op("vector", lambda e, pb=tile % 2, g_=g: e.tensor_tensor(
                    yb[pb][:], ps[pb][:], bias_bc[:, g_ * 512:(g_ + 1) * 512], ALU.add),
                    wait=stw, inc=(ev,))[ev]
            else:
                ev_ = ctx.op("vector", lambda e, pb=tile % 2: e.tensor_copy(
                    yb[pb][:], ps[pb][:]), wait=stw, inc=(ev,))[ev]
            ctx.op("gpsimd", lambda e, pb=tile % 2, g_=g, tt_=tt: e.dma_start(
                out=y_hbm[tt_ * 128:(tt_ + 1) * 128, g_ * 512:(g_ + 1) * 512],
                in_=yb[pb][:]),
                wait=((ev, ev_),), inc=(p + "st" + str(tile % 2),), dma=True)
            tile += 1
    return (p + "st0", ctx.val(p + "st0")), (p + "st1", ctx.val(p + "st1"))


def mk_wT_loader(w_hbm, KCH=NCH):
    def load(ctx, g, wbuf, sem):
        for c in range(KCH):
            ctx.op("sync", lambda e, c_=c, g_=g: e.dma_start_transpose(
                wbuf[c_][:], w_hbm[g_ * 512:(g_ + 1) * 512, c_ * 128:(c_ + 1) * 128]),
                inc=(sem,), dma=True)
        return sem, ctx.val(sem)
    return load


def mk_wK_loader(w_hbm, KCH):
    def load(ctx, g, wbuf, sem):
        for c in range(KCH):
            ctx.op("sync", lambda e, c_=c, g_=g: e.dma_start(
                out=wbuf[c_][:],
                in_=w_hbm[c_ * 128:(c_ + 1) * 128, g_ * 512:(g_ + 1) * 512]),
                inc=(sem,), dma=True)
        return sem, ctx.val(sem)
    return load


def emit_epilogue(ctx, y_hbm, x_hbm, xo_hbm, uo_hbm, p, rms=False,
                  n_tiles=T // 128, width=D, pre_wait=()):
    ybq = [ctx.sbuf(f"{p}qy{i}", [128, width], F32) for i in range(2)]
    xbq = [ctx.sbuf(f"{p}qx{i}", [128, width], F32) for i in range(2)]
    xob = [ctx.sbuf(f"{p}qo{i}", [128, width], F32) for i in range(2)]
    uob = [ctx.sbuf(f"{p}qu{i}", [128, width], BF16) for i in range(2)]
    nq = width // 512
    bn6 = ctx.sbuf(f"{p}qb6", [128, 6 * nq])
    mv = ctx.sbuf(f"{p}qmv", [128, 2])
    sd = ctx.sbuf(f"{p}qsd", [128, 1])
    rr = ctx.sbuf(f"{p}qrr", [128, 1])
    epsc = ctx.sbuf(f"{p}qep", [128, 1])
    epsr = ctx.sbuf(f"{p}qer", [128, 1])
    sq2 = ctx.sbuf(f"{p}qs2", [128, 1])
    ctx.op("vector", lambda e: e.memset(epsc[:], EPS_LN))
    ctx.op("vector", lambda e: e.memset(epsr[:], 1e-6))
    ctx.drain("vector")
    ld, smv, ssd, act, st = (p + s for s in ("ld", "mv", "sd", "ac", "wr"))

    def ln_cols(i, src, tag):
        for q in range(nq):
            ctx.op("vector", lambda e, q_=q, s_=src: e.bn_stats(
                bn6[:, q_ * 6:(q_ + 1) * 6], s_[:, q_ * 512:(q_ + 1) * 512]))
        ctx.drain("vector")
        mvv = ctx.op("vector", lambda e: e.bn_aggr(mv[:], bn6[:]), inc=(smv,))[smv]
        sdv = ctx.op("scalar", lambda e: e.activation(sd[:], mv[:, 1:2], AF.Sqrt,
                                                      bias=epsc[:]),
                     wait=((smv, mvv),), inc=(ssd,))[ssd]
        ctx.op("vector", lambda e: e.reciprocal(rr[:], sd[:]), wait=((ssd, sdv),))
        ctx.drain("vector")

    for i in range(n_tiles):
        bi = i % 2
        w = list(pre_wait if i == 0 else ())
        if i >= 2:
            w.append((act, 2 * (i - 1)))
        lv = ctx.op("sync", lambda e, b=bi, i_=i: e.dma_start(
            out=ybq[b][:], in_=y_hbm[i_ * 128:(i_ + 1) * 128, :]),
            wait=w, inc=(ld + str(bi),), dma=True)[ld + str(bi)]
        lv = ctx.op("sync", lambda e, b=bi, i_=i: e.dma_start(
            out=xbq[b][:], in_=x_hbm[i_ * 128:(i_ + 1) * 128, :]),
            inc=(ld + str(bi),), dma=True)[ld + str(bi)]
        ctx.op("vector", lambda e: e.nop(), wait=((ld + str(bi), lv),))
        if rms:
            for q in range(nq):
                ctx.op("vector", lambda e, q_=q, b=bi: e.bn_stats(
                    bn6[:, q_ * 6:(q_ + 1) * 6], ybq[b][:, q_ * 512:(q_ + 1) * 512]))
            ctx.drain("vector")
            ctx.op("vector", lambda e: e.bn_aggr(mv[:], bn6[:]))
            ctx.drain("vector")
            ctx.op("vector", lambda e: e.tensor_tensor(sq2[:], mv[:, 0:1], mv[:, 0:1],
                                                       ALU.mult))
            ctx.drain("vector")
            ctx.op("vector", lambda e: e.tensor_tensor(sq2[:], sq2[:], mv[:, 1:2],
                                                       ALU.add), inc=(p + "r2",))
            sdv = ctx.op("scalar", lambda e: e.activation(
                sd[:], sq2[:], AF.Sqrt, bias=epsr[:]),
                wait=((p + "r2", ctx.val(p + "r2")),), inc=(ssd,))[ssd]
            ctx.op("vector", lambda e: e.reciprocal(rr[:], sd[:]),
                   wait=((ssd, ctx.val(ssd)),))
            ctx.drain("vector")
            ctx.op("vector", lambda e, b=bi: e.tensor_scalar(
                ybq[b][:], ybq[b][:], rr[:], None, ALU.mult))
            ctx.drain("vector")
        ln_cols(i, ybq[bi], "y")
        ctx.op("vector", lambda e, b=bi: e.tensor_scalar(
            ybq[b][:], ybq[b][:], mv[:, 0:1], rr[:], ALU.subtract, ALU.mult))
        ctx.drain("vector")
        wst = []
        if i >= 2:
            wst.append((st + str(i % 2), 32 * (i // 2)))
        ctx.op("vector", lambda e, b=bi: e.tensor_tensor(
            xob[b][:], xbq[b][:], ybq[b][:], ALU.add), wait=wst, inc=(act,))
        ctx.drain("vector")
        ln_cols(i, xob[bi], "x")
        av = ctx.op("vector", lambda e, b=bi: e.tensor_scalar(
            uob[b][:], xob[b][:], mv[:, 0:1], rr[:], ALU.subtract, ALU.mult),
            inc=(act,))[act]
        ctx.op("gpsimd", lambda e, b=bi, i_=i: e.dma_start(
            out=xo_hbm[i_ * 128:(i_ + 1) * 128, :], in_=xob[b][:]),
            wait=((act, av),), inc=(st + str(bi),), dma=True)
        ctx.op("gpsimd", lambda e, b=bi, i_=i: e.dma_start(
            out=uo_hbm[i_ * 128:(i_ + 1) * 128, :], in_=uob[b][:]),
            inc=(st + str(bi),), dma=True)
    return (st + "0", ctx.val(st + "0")), (st + "1", ctx.val(st + "1"))


def launch_l0d(att_list, wo0b, x0_shards):
    nc = bass.Bass("TRN2", target_bir_lowering=False)
    ctx = Ctx(nc)
    att_tm = dram_in(nc, "att_tm", [T, D], BF16)
    wo0b_t = dram_in(nc, "wo0b", [D, D], BF16)
    x0 = dram_in(nc, "x0", [T, D])
    x1 = dram_out(nc, "x1", [T, D])
    u1 = dram_out(nc, "u1_tm", [T, D], BF16)
    att_fm = nc.dram_tensor("att_fm", [D, T], BF16, kind="Internal").ap()
    y_tm = nc.dram_tensor("y_tm", [T, D], F32, kind="Internal").ap()
    pw = emit_tm_to_fm(ctx, att_tm, att_fm, "tf")
    sw = emit_out_proj(ctx, {}, "op", att_fm, mk_wT_loader(wo0b_t), y_tm, D,
                       pre_wait=pw)
    ew = emit_epilogue(ctx, y_tm, x0, x1, u1, "ep", pre_wait=sw)
    ctx.op("sync", lambda e: e.nop(), wait=ew)
    ctx.build()
    maps = [{"att_tm": att_list[c], "wo0b": wo0b, "x0": x0_shards[c]}
            for c in range(NCORES)]
    res = run_bass_kernel_spmd(nc, maps, core_ids=list(range(NCORES)))
    return res.results


def emit_h128(ctx):
    ia = ctx.sbuf("hia", [128, 128], I32)
    ib = ctx.sbuf("hib", [128, 128], I32)
    hx = ctx.sbuf("hx", [128, 128], I32)
    ht = ctx.sbuf("ht", [128, 128], I32)
    hf = ctx.sbuf("hf", [128, 128], F32)
    h128 = ctx.sbuf("h128", [128, 128], BF16)
    ctx.op("gpsimd", lambda e: e.iota(ia[:], [[0, 128]], channel_multiplier=1))
    ctx.op("gpsimd", lambda e: e.iota(ib[:], [[1, 128]], channel_multiplier=0),
           inc=("hcst",))
    ctx.op("vector", lambda e: e.tensor_tensor(hx[:], ia[:], ib[:], ALU.bitwise_and),
           wait=(("hcst", 1),))
    ctx.drain("vector")
    for sh in (1, 2, 4):
        ctx.op("vector", lambda e, s_=sh: e.tensor_scalar(
            ht[:], hx[:], s_, None, ALU.logical_shift_right))
        ctx.drain("vector")
        ctx.op("vector", lambda e: e.tensor_tensor(hx[:], hx[:], ht[:],
                                                   ALU.bitwise_xor))
        ctx.drain("vector")
    ctx.op("vector", lambda e: e.tensor_scalar(hx[:], hx[:], 1, None,
                                               ALU.bitwise_and))
    ctx.drain("vector")
    ctx.op("vector", lambda e: e.tensor_scalar(hx[:], hx[:], -2, 1,
                                               ALU.mult, ALU.add))
    ctx.drain("vector")
    ctx.op("vector", lambda e: e.tensor_copy(hf[:], hx[:]))
    ctx.drain("vector")
    ctx.op("vector", lambda e: e.tensor_copy(h128[:], hf[:]), inc=("h128d",))
    ctx.drain("vector")
    return h128, ("h128d", ctx.val("h128d"))


def emit_su_cols(ctx, su_hbm, n_chunk, p, eye_b, eyew):
    r = ctx.sbuf(p + "r", [n_chunk, 128], F32)
    rb = ctx.sbuf(p + "rb", [n_chunk, 128], BF16)
    cols = ctx.sbuf(p + "c", [128, n_chunk], F32)
    pt = ctx.psum(p + "pt", [128, n_chunk], BF16)
    lv = ctx.op("sync", lambda e: e.dma_start(
        out=r[:], in_=su_hbm.rearrange("(a b) -> a b", b=128)),
        inc=(p + "ld",), dma=True)[p + "ld"]
    ctx.op("vector", lambda e: e.tensor_copy(rb[:], r[:]),
           wait=((p + "ld", lv),), inc=(p + "cb",))
    ctx.drain("vector")
    ctx.op("tensor", lambda e: e.transpose(
        pt[:], rb[:], eye_b[0:n_chunk, 0:n_chunk]),
        wait=[(p + "cb", 1), eyew], inc=(p + "tt",))
    ctx.op("vector", lambda e: e.tensor_copy(cols[:], pt[:]),
           wait=((p + "tt", 1),), inc=(p + "cd",))
    ctx.drain("vector")
    return cols, (p + "cd", ctx.val(p + "cd"))


def emit_hfold(ctx, p, w_nat_b, out_T, K_in, K_out, h128, h128w, su_cols, suw,
               bufs=None, pre_wait=()):
    NCHK = K_in // 128
    n_grp = K_out // 512
    nbits = NCHK.bit_length() - 1
    wt, za, zb, ob, psh = bufs
    mm, ev, bt, st = p + "mm", p + "ev", p + "bt", p + "st"
    for g in range(n_grp):
        lw = [(p + "sv", g * NCHK)] if g >= 1 else list(pre_wait)
        for c in range(NCHK):
            ctx.op("sync", lambda e, c_=c, g_=g: e.dma_start_transpose(
                wt[c_][:], w_nat_b[g_ * 512:(g_ + 1) * 512, c_ * 128:(c_ + 1) * 128]),
                wait=(lw if c == 0 else ()), inc=(p + "wl",), dma=True)
        wlv = ctx.val(p + "wl")
        for c in range(NCHK):
            pb = c % 2
            mw = [(p + "wl", wlv)] if c == 0 else []
            if g > 0 and c < 2:
                mw.append((p + "sv", g * NCHK))
            if c >= 2:
                mw.append((ev, g * NCHK + c - 1))
            ctx.op("tensor", lambda e, c_=c, pb_=pb: e.matmul(
                psh[pb_][:], h128[:], wt[c_][:], start=True, stop=True),
                wait=mw + ([h128w] if g == 0 and c == 0 else []), inc=(mm,))
            ctx.op("vector", lambda e, c_=c, pb_=pb: e.tensor_copy(
                za[c_][:], psh[pb_][:]), wait=((mm, ctx.val(mm)),), inc=(ev,))
            ctx.drain("vector")
        cur, nxt = za, zb
        for bit in range(nbits):
            step = 1 << bit
            for c0 in range(NCHK):
                if c0 & step:
                    continue
                c1 = c0 | step
                ctx.op("vector", lambda e, o_=nxt[c0], x_=cur[c0], y_=cur[c1]:
                       e.tensor_tensor(o_[:], x_[:], y_[:], ALU.add))
                ctx.op("vector", lambda e, o_=nxt[c1], x_=cur[c0], y_=cur[c1]:
                       e.tensor_tensor(o_[:], x_[:], y_[:], ALU.subtract))
            ctx.drain("vector")
            cur, nxt = nxt, cur
        for c in range(NCHK):
            oi = g * NCHK + c
            w = [suw] if oi == 0 else []
            if oi >= 2:
                w.append((st + str(oi % 2), 16 * (oi // 2)))
            sv = ctx.op("vector", lambda e, c_=c, ob_=ob[oi % 2], cc=cur[c]:
                        e.tensor_scalar(
                ob_[:], cc[:], su_cols[:, c_:c_ + 1], None, ALU.mult),
                wait=w, inc=(p + "sv",))[p + "sv"]
            ctx.op("gpsimd", lambda e, c_=c, g_=g, ob_=ob[oi % 2]: e.dma_start(
                out=out_T[c_ * 128:(c_ + 1) * 128, g_ * 512:(g_ + 1) * 512],
                in_=ob_[:]),
                wait=((p + "sv", sv),), inc=(st + str(oi % 2),), dma=True)
    return (st + "0", ctx.val(st + "0")), (st + "1", ctx.val(st + "1"))


def launch_stage_h(inputs):
    nc = bass.Bass("TRN2", target_bir_lowering=False)
    ctx = Ctx(nc)
    W_up = dram_in(nc, "W_up", [I0, D])
    W_gate = dram_in(nc, "W_gate", [I0, D])
    W_down = dram_in(nc, "W_down", [D, I0])
    SU_ug = dram_in(nc, "SU_ug", [D])
    SU_dp = dram_in(nc, "SU_dp", [I0])
    Mup_T = dram_out(nc, "Mup_T", [D, I0], BF16)
    Mgate_T = dram_out(nc, "Mgate_T", [D, I0], BF16)
    Mdn_T = dram_out(nc, "Mdn_T", [I0, D], BF16)
    wupb = nc.dram_tensor("wupb", [I0, D], BF16, kind="Internal").ap()
    wgab = nc.dram_tensor("wgab", [I0, D], BF16, kind="Internal").ap()
    wdnb = nc.dram_tensor("wdnb", [D, I0], BF16, kind="Internal").ap()
    for dst, srcw in ((wupb, W_up), (wgab, W_gate), (wdnb, W_down)):
        ctx.op("gpsimd", lambda e, a=dst, b=srcw: e.dma_start(out=a, in_=b),
               inc=("cst",), dma=True)
    CST = ctx.val("cst")
    ctx.op("sync", lambda e: e.nop(), wait=(("cst", CST),))
    h128, h128w = emit_h128(ctx)
    ia2 = ctx.sbuf("ia2", [128, 128], I32)
    ib2 = ctx.sbuf("ib2", [128, 128], I32)
    ey2 = ctx.sbuf("ey2", [128, 128], F32)
    eye2 = ctx.sbuf("eye2", [128, 128], BF16)
    ctx.op("gpsimd", lambda e: e.iota(ia2[:], [[0, 128]], channel_multiplier=1))
    ctx.op("gpsimd", lambda e: e.iota(ib2[:], [[1, 128]], channel_multiplier=0),
           inc=("ic2",))
    ctx.op("vector", lambda e: e.tensor_tensor(ey2[:], ia2[:], ib2[:], ALU.is_equal),
           wait=(("ic2", 1),))
    ctx.drain("vector")
    ctx.op("vector", lambda e: e.tensor_copy(eye2[:], ey2[:]), inc=("ey2",))
    ctx.drain("vector")
    eyew = ("ey2", 1)
    suu, suuw = emit_su_cols(ctx, SU_ug, NCH, "suu", eye2, eyew)
    sud, sudw = emit_su_cols(ctx, SU_dp, 64, "sud", eye2, eyew)
    wtP = [ctx.sbuf(f"hwt{c}", [128, 512], BF16, is_transpose_dest=True)
           for c in range(64)]
    zaP = [ctx.sbuf(f"hza{c}", [128, 512], BF16) for c in range(64)]
    zbP = [ctx.sbuf(f"hzb{c}", [128, 512], BF16) for c in range(64)]
    obP = [ctx.sbuf(f"hob{i}", [128, 512], BF16) for i in range(2)]
    pshP = [ctx.psum(f"hph{i}", [128, 512], F32) for i in range(2)]
    bufs = (wtP, zaP, zbP, obP, pshP)
    w1 = emit_hfold(ctx, "hu", wupb, Mup_T, D, I0, h128, h128w, suu, suuw, bufs)
    w2 = emit_hfold(ctx, "hg", wgab, Mgate_T, D, I0, h128, h128w, suu, suuw, bufs,
                    pre_wait=[("humm", 256), ("huev", 256)])
    w3 = emit_hfold(ctx, "hd", wdnb, Mdn_T, I0, D, h128, h128w, sud, sudw, bufs,
                    pre_wait=[("hgmm", 256), ("hgev", 256)])
    dbg_h = dram_out(nc, "dbg_h", [128, 128], F32)
    hfo = ctx.sbuf("hfo", [128, 128], F32)
    ctx.op("vector", lambda e: e.tensor_copy(hfo[:], h128[:]), wait=[h128w],
           inc=("dbh",))
    ctx.op("gpsimd", lambda e: e.dma_start(out=dbg_h, in_=hfo[:]),
           wait=(("dbh", 1),), inc=("dbh2",), dma=True)
    dbg_su = dram_out(nc, "dbg_su", [128, 16], F32)
    ctx.op("gpsimd", lambda e: e.dma_start(out=dbg_su, in_=suu[:]),
           wait=[suuw], inc=("dbh2",), dma=True)
    ctx.op("sync", lambda e: e.nop(), wait=tuple(list(w1) + list(w2) + list(w3) + [("dbh2", 32)]))
    ctx.build()
    m = {"W_up": inputs["L0_ffn_W_up"], "W_gate": inputs["L0_ffn_W_gate"],
         "W_down": inputs["L0_ffn_W_down"], "SU_ug": inputs["L0_ffn_SU_ug"],
         "SU_dp": inputs["L0_ffn_SU_dp"]}
    res = run_bass_kernel_spmd(nc, [m] * NCORES, core_ids=list(range(NCORES)))
    return res.results


def emit_eye(ctx, p):
    ia = ctx.sbuf(p + "ia", [128, 128], I32)
    ib = ctx.sbuf(p + "ib", [128, 128], I32)
    eyf = ctx.sbuf(p + "ef", [128, 128], F32)
    eyb = ctx.sbuf(p + "eb", [128, 128], BF16)
    ctx.op("gpsimd", lambda e: e.iota(ia[:], [[0, 128]], channel_multiplier=1))
    ctx.op("gpsimd", lambda e: e.iota(ib[:], [[1, 128]], channel_multiplier=0),
           inc=(p + "i",))
    ctx.op("vector", lambda e: e.tensor_tensor(eyf[:], ia[:], ib[:], ALU.is_equal),
           wait=((p + "i", 1),))
    ctx.drain("vector")
    ctx.op("vector", lambda e: e.tensor_copy(eyb[:], eyf[:]), inc=(p + "d",))
    ctx.drain("vector")
    return eyf, eyb, (p + "d", ctx.val(p + "d"))


def emit_vec_cols_f32(ctx, v_hbm, n_chunk, p, eye_f, eyew, scale=1.0):
    r = ctx.sbuf(p + "r", [n_chunk, 128], F32)
    cols = ctx.sbuf(p + "c", [128, n_chunk], F32)
    pt = ctx.psum(p + "pt", [128, n_chunk], F32)
    lv = ctx.op("sync", lambda e: e.dma_start(
        out=r[:], in_=v_hbm.rearrange("(a b) -> a b", b=128)),
        inc=(p + "ld",), dma=True)[p + "ld"]
    ctx.op("tensor", lambda e: e.transpose(
        pt[:], r[:], eye_f[0:n_chunk, 0:n_chunk]),
        wait=[(p + "ld", lv), eyew], inc=(p + "tt",))
    ctx.op("vector", lambda e: e.tensor_scalar(cols[:], pt[:], scale, None, ALU.mult),
           wait=((p + "tt", 1),), inc=(p + "cd",))
    ctx.drain("vector")
    return cols, (p + "cd", ctx.val(p + "cd"))


def emit_glu_proj(ctx, p, u_res, uldw, loadA, loadB, n_pairs, h_fm,
                  sA_cols=None, sB_cols=None, s_wait=()):
    KCH = NCH
    n_grp = n_pairs // 4
    wbufA = [[ctx.sbuf(f"{p}A{i}_{c}", [128, 512], BF16, is_transpose_dest=True)
              for c in range(KCH)] for i in range(2)]
    wbufB = [[ctx.sbuf(f"{p}B{i}_{c}", [128, 512], BF16, is_transpose_dest=True)
              for c in range(KCH)] for i in range(2)]
    psA = [ctx.psum(f"{p}pA{i}", [128, 512], F32) for i in range(2)]
    psB = [ctx.psum(f"{p}pB{i}", [128, 512], F32) for i in range(2)]
    sil = [ctx.sbuf(f"{p}sl{i}", [128, 512], F32) for i in range(2)]
    hb = [ctx.sbuf(f"{p}h{i}", [128, 512], BF16) for i in range(2)]
    mmA, mmB, sa, hv = p + "mA", p + "mB", p + "sa", p + "hv"
    NTT = T // 512
    tile = 0
    for g in range(n_grp):
        bi = g % 2
        if g >= 2:
            ctx.op("sync", lambda e: e.nop(), wait=((mmB, (g - 1) * 4 * NTT),))
        wldA, wldvA = loadA(ctx, g, wbufA[bi], f"{p}la{g}")
        wldB, wldvB = loadB(ctx, g, wbufB[bi], f"{p}lb{g}")
        for os_ in range(4):
            ot = g * 4 + os_
            for tt in range(NTT):
                pb = tile % 2
                twA = [(wldA, wldvA)] + list(s_wait if tile == 0 else ())
                if tile == 0:
                    twA.append(uldw)
                if tile >= 2:
                    twA.append((sa, tile - 1))
                for c in range(KCH):
                    ctx.op("tensor", lambda e, c_=c, pb_=pb, bi_=bi, os2=os_, tt_=tt:
                           e.matmul(psA[pb_][:],
                                    wbufA[bi_][c_][:, os2 * 128:(os2 + 1) * 128],
                                    u_res[c_][:, tt_ * 512:(tt_ + 1) * 512],
                                    start=(c_ == 0), stop=(c_ == KCH - 1)),
                           wait=(twA if c == 0 else ()),
                           inc=((mmA,) if c == KCH - 1 else ()))
                twB = [(wldB, wldvB)]
                if tile >= 2:
                    twB.append((hv, tile - 1))
                for c in range(KCH):
                    ctx.op("tensor", lambda e, c_=c, pb_=pb, bi_=bi, os2=os_, tt_=tt:
                           e.matmul(psB[pb_][:],
                                    wbufB[bi_][c_][:, os2 * 128:(os2 + 1) * 128],
                                    u_res[c_][:, tt_ * 512:(tt_ + 1) * 512],
                                    start=(c_ == 0), stop=(c_ == KCH - 1)),
                           wait=(twB if c == 0 else ()),
                           inc=((mmB,) if c == KCH - 1 else ()))
                if sA_cols is not None:
                    sv = ctx.op("scalar", lambda e, pb_=pb, ot_=ot: e.activation(
                        sil[pb_][:], psA[pb_][:], AF.Silu,
                        scale=sA_cols[:, ot_:ot_ + 1]),
                        wait=((mmA, ctx.val(mmA)),), inc=(sa,))[sa]
                else:
                    sv = ctx.op("scalar", lambda e, pb_=pb: e.activation(
                        sil[pb_][:], psA[pb_][:], AF.Silu),
                        wait=((mmA, ctx.val(mmA)),), inc=(sa,))[sa]
                hw = [(sa, sv), (mmB, ctx.val(mmB))]
                if tile >= 2:
                    hw.append((p + "st" + str(tile % 2), 16 * (tile // 2)))
                if sB_cols is not None:
                    hvv = ctx.op("vector", lambda e, pb_=pb, ot_=ot:
                                 e.scalar_tensor_tensor(
                                     hb[pb_][:], psB[pb_][:], sB_cols[:, ot_:ot_ + 1],
                                     sil[pb_][:], ALU.mult, ALU.mult),
                                 wait=hw, inc=(hv,))[hv]
                else:
                    hvv = ctx.op("vector", lambda e, pb_=pb: e.tensor_tensor(
                        hb[pb_][:], psB[pb_][:], sil[pb_][:], ALU.mult),
                        wait=hw, inc=(hv,))[hv]
                ctx.op("gpsimd", lambda e, pb_=pb, ot_=ot, tt_=tt: e.dma_start(
                    out=h_fm[ot_ * 128:(ot_ + 1) * 128, tt_ * 512:(tt_ + 1) * 512],
                    in_=hb[pb_][:]),
                    wait=((hv, hvv),), inc=(p + "st" + str(tile % 2),), dma=True)
                tile += 1
    return (p + "st0", ctx.val(p + "st0")), (p + "st1", ctx.val(p + "st1"))


def launch_l0e(u1_list, MupT, MgateT, inputs):
    nc = bass.Bass("TRN2", target_bir_lowering=False)
    ctx = Ctx(nc)
    u1_tm = dram_in(nc, "u1_tm", [T, D], BF16)
    Mup = dram_in(nc, "MupT", [D, I0], BF16)
    Mga = dram_in(nc, "MgaT", [D, I0], BF16)
    Wsc = dram_in(nc, "Wsc", [2 * I0])
    h_fm = dram_out(nc, "h_fm", [I0, T], BF16)
    eyf, eyb, eyw = emit_eye(ctx, "ey")
    wsc_cols, wscw = emit_vec_cols_f32(ctx, Wsc, 128, "ws", eyf, eyw, scale=RS2048)
    u_res = [ctx.sbuf(f"u{c}", [128, T], BF16, is_transpose_dest=True)
             for c in range(NCH)]
    uld, uldv = emit_load_u_fm(ctx, u1_tm, u_res, "ul")
    sup = wsc_cols.ap()[:, 0:64]
    sga = wsc_cols.ap()[:, 64:128]
    sw = emit_glu_proj(ctx, "gl", u_res, (uld, uldv),
                       mk_wK_loader(Mga, NCH), mk_wK_loader(Mup, NCH),
                       I0 // 128, h_fm, sA_cols=sga, sB_cols=sup, s_wait=[wscw])
    ctx.op("sync", lambda e: e.nop(), wait=sw)
    ctx.build()
    maps = [{"u1_tm": u1_list[c], "MupT": MupT, "MgaT": MgateT,
             "Wsc": inputs["L0_ffn_Wscale_ug"]} for c in range(NCORES)]
    res = run_bass_kernel_spmd(nc, maps, core_ids=list(range(NCORES)))
    return res.results


def launch_l0f(h_list, MdnT, x1_list, inputs):
    nc = bass.Bass("TRN2", target_bir_lowering=False)
    ctx = Ctx(nc)
    h_fm = dram_in(nc, "h_fm", [I0, T], BF16)
    Mdn = dram_in(nc, "MdnT", [I0, D], BF16)
    Wsd = dram_in(nc, "Wsd", [1, D])
    x1 = dram_in(nc, "x1", [T, D])
    x2 = dram_out(nc, "x2", [T, D])
    u2 = dram_out(nc, "u2_tm", [T, D], BF16)
    y_tm = nc.dram_tensor("y_tm", [T, D], F32, kind="Internal").ap()
    eyf, eyb, eyw = emit_eye(ctx, "ey")
    wr = ctx.sbuf("wr", [1, D], F32)
    wrb = ctx.sbuf("wrb", [1, D], BF16)
    onesr = ctx.sbuf("onesr", [1, 128], BF16)
    sdn = ctx.sbuf("sdn", [128, D], F32)
    psd = ctx.psum("psd", [128, 512], F32)
    ctx.op("vector", lambda e: e.memset(onesr[:], 1.0))
    ctx.drain("vector")
    lv = ctx.op("sync", lambda e: e.dma_start(out=wr[:], in_=Wsd),
                inc=("wl",), dma=True)["wl"]
    ctx.op("vector", lambda e: e.tensor_scalar(wrb[:], wr[:], RS8192, None, ALU.mult),
           wait=(("wl", lv),), inc=("wb",))
    ctx.drain("vector")
    for qq in range(4):
        ctx.op("tensor", lambda e, q_=qq: e.matmul(
            psd[:], onesr[:], wrb[:, q_ * 512:(q_ + 1) * 512], start=True, stop=True),
            wait=(("wb", 1),) if qq == 0 else (("sde", qq),), inc=("sdm",))
        ctx.op("vector", lambda e, q_=qq: e.tensor_copy(
            sdn[:, q_ * 512:(q_ + 1) * 512], psd[:]),
            wait=(("sdm", qq + 1),), inc=("sde",))
        ctx.drain("vector")
    sdw = ("sde", 4)
    sw = emit_out_proj(ctx, {}, "op", h_fm, mk_wK_loader(Mdn, I0 // 128), y_tm,
                       I0, scale_bc=sdn, pre_wait=[sdw])
    ew = emit_epilogue(ctx, y_tm, x1, x2, u2, "ep", pre_wait=sw)
    ctx.op("sync", lambda e: e.nop(), wait=ew)
    ctx.build()
    maps = [{"h_fm": h_list[c], "MdnT": MdnT, "x1": x1_list[c],
             "Wsd": inputs["L0_ffn_Wscale_dp"].reshape(1, D)} for c in range(NCORES)]
    res = run_bass_kernel_spmd(nc, maps, core_ids=list(range(NCORES)))
    return res.results


def emit_rope_tables(ctx, p):
    th = ctx.sbuf(p + "th", [128, 1], F32)
    sg = ctx.sbuf(p + "sg", [128, 1], F32)
    cos_t = ctx.sbuf(p + "cs", [128, S], F32)
    sin_t = ctx.sbuf(p + "sn", [128, S], F32)
    ses = ExitStack()
    sb2 = lambda n, sh, dt=F32: ses.enter_context(ctx.nc.sbuf_tensor(n, sh, dt))
    ic = sb2(p + "ic", [128, 1], I32)
    ic2 = sb2(p + "ic2", [128, 1], I32)
    it = sb2(p + "it", [128, S], I32)
    itf = sb2(p + "itf", [128, S], F32)
    ang = sb2(p + "an", [128, S], F32)
    red = sb2(p + "rd", [128, S], F32)
    ni = sb2(p + "ni", [128, S], I32)
    ge = sb2(p + "ge", [128, S], F32)
    TWO_PI = 6.283185307179586
    ctx.op("gpsimd", lambda e: e.iota(ic[:], [[0, 1]], channel_multiplier=1))
    ctx.op("gpsimd", lambda e: e.iota(it[:], [[1, S]], channel_multiplier=0),
           inc=(p + "io",))
    ctx.op("vector", lambda e: e.tensor_scalar(ic2[:], ic[:], 1, None,
                                               ALU.logical_shift_right),
           wait=((p + "io", 1),))
    ctx.drain("vector")
    ctx.op("vector", lambda e: e.tensor_copy(th[:], ic2[:]))
    ctx.drain("vector")
    ctx.op("vector", lambda e: e.tensor_scalar(ic2[:], ic[:], 1, None,
                                               ALU.bitwise_and))
    ctx.drain("vector")
    ctx.op("vector", lambda e: e.tensor_scalar(ic2[:], ic2[:], 2, -1,
                                               ALU.mult, ALU.add))
    ctx.drain("vector")
    ctx.op("vector", lambda e: e.tensor_copy(sg[:], ic2[:]), inc=(p + "sgd",))
    C = -float(np.log(10000.0) / 64.0)
    ctx.op("scalar", lambda e: e.activation(th[:], th[:], AF.Exp, scale=C),
           wait=((p + "sgd", 1),), inc=(p + "th",))
    ctx.op("vector", lambda e: e.tensor_copy(itf[:], it[:]), wait=((p + "th", 1),))
    ctx.drain("vector")

    def reduce_and_sin(dst, shift):
        if shift == 0.0:
            ctx.op("vector", lambda e: e.tensor_scalar(ang[:], itf[:], th[:], None,
                                                       ALU.mult))
        else:
            ctx.op("vector", lambda e, s_=shift: e.tensor_scalar(
                ang[:], itf[:], th[:], s_, ALU.mult, ALU.add))
        ctx.drain("vector")
        ctx.op("vector", lambda e: e.tensor_scalar(
            red[:], ang[:], 1.0 / TWO_PI, 1024.0, ALU.mult, ALU.add))
        ctx.drain("vector")
        ctx.op("vector", lambda e: e.tensor_copy(ni[:], red[:]))
        ctx.drain("vector")
        ctx.op("vector", lambda e: e.tensor_copy(red[:], ni[:]))
        ctx.drain("vector")
        ctx.op("vector", lambda e: e.tensor_scalar(
            red[:], red[:], -TWO_PI, 1024.0 * TWO_PI, ALU.mult, ALU.add))
        ctx.drain("vector")
        ctx.op("vector", lambda e: e.tensor_tensor(red[:], ang[:], red[:], ALU.add))
        ctx.drain("vector")
        ctx.op("vector", lambda e: e.tensor_scalar(ge[:], red[:], 3.14159265, None,
                                                   ALU.is_ge))
        ctx.drain("vector")
        ctx.op("vector", lambda e: e.scalar_tensor_tensor(
            red[:], ge[:], -TWO_PI, red[:], ALU.mult, ALU.add), inc=(p + "rr",))
        ctx.op("scalar", lambda e, d_=dst: e.activation(d_[:], red[:], AF.Sin),
               wait=((p + "rr", ctx.val(p + "rr")),), inc=(p + "sf",))
        ctx.op("vector", lambda e: e.nop(), wait=((p + "sf", ctx.val(p + "sf")),))

    reduce_and_sin(cos_t, float(np.pi / 2))
    reduce_and_sin(sin_t, 0.0)
    ctx.op("vector", lambda e: e.tensor_scalar(sin_t[:], sin_t[:], sg[:], None,
                                               ALU.mult), inc=(p + "done",))
    ctx.drain("vector")
    ses.close()
    return cos_t, sin_t, (p + "done", ctx.val(p + "done"))


def build_l1_proj(ctx, io):
    u_hbm = io["u2_tm"]
    wq, wk, wv = io["wq1b"], io["wk1b"], io["wv1b"]
    bq, bk, bv = io["bq"], io["bk"], io["bv"]
    q_fm, k_fm, v_tm = io["q_fm"], io["k_fm"], io["v_tm"]

    cos_t, sin_t, ropew = emit_rope_tables(ctx, "rp")
    for _eng in Ctx.ENGS:
        ctx.op(_eng, lambda e: e.nop(), wait=(ropew,))
    eyf, eyb, eyw = emit_eye(ctx, "ey")
    bqc, bqw = emit_vec_cols_f32(ctx, bq, NCH, "bq", eyf, eyw)
    bkc, bkw = emit_vec_cols_f32(ctx, bk, NCH, "bk", eyf, eyw)
    onesr = ctx.sbuf("onesr1", [1, 128], BF16)
    bvbc = ctx.sbuf("bvbc", [128, D], F32)
    bes = ExitStack()
    bvr = bes.enter_context(ctx.nc.sbuf_tensor("bvr", [1, D], F32))
    bvrb = bes.enter_context(ctx.nc.sbuf_tensor("bvrb", [1, D], BF16))
    psb = ctx.psum("psb", [128, 512], F32)
    ctx.op("vector", lambda e: e.memset(onesr[:], 1.0))
    ctx.drain("vector")
    lv = ctx.op("sync", lambda e: e.dma_start(out=bvr[:], in_=bv), inc=("bvl",),
                dma=True)["bvl"]
    ctx.op("vector", lambda e: e.tensor_copy(bvrb[:], bvr[:]), wait=(("bvl", lv),),
           inc=("bvb",))
    ctx.drain("vector")
    for qq in range(4):
        ctx.op("tensor", lambda e, q_=qq: e.matmul(
            psb[:], onesr[:], bvrb[:, q_ * 512:(q_ + 1) * 512], start=True, stop=True),
            wait=(("bvb", 1),) if qq == 0 else (("bve", qq),), inc=("bvm",))
        ctx.op("vector", lambda e, q_=qq: e.tensor_copy(
            bvbc[:, q_ * 512:(q_ + 1) * 512], psb[:]),
            wait=(("bvm", qq + 1),), inc=("bve",))
        ctx.drain("vector")
    bes.close()

    u_res = [ctx.sbuf(f"u{c}", [128, T], BF16, is_transpose_dest=True)
             for c in range(NCH)]
    uld, uldv = emit_load_u_fm(ctx, u_hbm, u_res, "l1p",
                               pre_wait=[ropew, ("bve", 4)])

    wbuf = [[ctx.sbuf(f"w{i}_{c}", [128, 512], BF16) for c in range(NCH)]
            for i in range(2)]
    ps = [ctx.psum(f"pp{i}", [128, 512], F32) for i in range(2)]
    qf = [ctx.sbuf(f"qf{i}", [128, 512], F32) for i in range(2)]
    xs = [ctx.sbuf(f"xs{i}", [128, 512], F32) for i in range(2)]
    ob = [ctx.sbuf(f"ob{i}", [128, 512], BF16) for i in range(2)]
    mm, ev2, wst = "mmq", "ev2", "wst"
    gidx = 0
    tile = 0
    SHUF = []
    for i2 in range(16):
        SHUF += [2 * i2 + 1, 2 * i2]

    def proj_qk(w_hbm, bcols, bw, out_fm):
        nonlocal gidx, tile
        for grp in range(4):
            bi = gidx % 2
            if gidx >= 2:
                ctx.op("sync", lambda e: e.nop(), wait=((mm, 32 * (gidx - 1)),))
            wld, wldv = mk_wK_loader(w_hbm, NCH)(ctx, grp, wbuf[bi], f"g{gidx}")
            for tt in range(8):
                for es in range(4):
                    e_tile = grp * 4 + es
                    pb = tile % 2
                    tw = [(wld, wldv), bw, ropew]
                    if tile >= 2:
                        tw.append((ev2, tile - 1))
                    if tile == 0:
                        tw.append((uld, uldv))
                    for c in range(NCH):
                        r = ctx.op("tensor", lambda e, c_=c, pb_=pb, bi_=bi, es_=es,
                                   tt_=tt: e.matmul(
                            ps[pb_][:], wbuf[bi_][c_][:, es_ * 128:(es_ + 1) * 128],
                            u_res[c_][:, tt_ * 512:(tt_ + 1) * 512],
                            start=(c_ == 0), stop=(c_ == NCH - 1)),
                            wait=(tw if c == 0 else ()),
                            inc=((mm,) if c == NCH - 1 else ()))
                    mmv = r[mm]
                    ctx.op("vector", lambda e, pb_=pb, e_=e_tile: e.tensor_scalar(
                        qf[pb_][:], ps[pb_][:], bcols[:, e_:e_ + 1], None, ALU.add),
                        wait=((mm, mmv),))
                    ctx.drain("vector")
                    ctx.op("vector", lambda e, pb_=pb: e.stream_shuffle(
                        xs[pb_][:], qf[pb_][:], SHUF))
                    ctx.drain("vector")
                    tcol = (tt % 4) * 512
                    ctx.op("vector", lambda e, pb_=pb, tc=tcol: e.tensor_tensor(
                        qf[pb_][:], qf[pb_][:], cos_t[:, tc:tc + 512], ALU.mult))
                    ctx.drain("vector")
                    ctx.op("vector", lambda e, pb_=pb, tc=tcol: e.tensor_tensor(
                        xs[pb_][:], xs[pb_][:], sin_t[:, tc:tc + 512], ALU.mult))
                    ctx.drain("vector")
                    stw = []
                    if tile >= 2:
                        stw.append((wst + str(tile % 2), 16 * (tile // 2)))
                    v2 = ctx.op("vector", lambda e, pb_=pb: e.tensor_tensor(
                        ob[pb_][:], qf[pb_][:], xs[pb_][:], ALU.add),
                        wait=stw, inc=(ev2,))[ev2]
                    ctx.op("gpsimd", lambda e, pb_=pb, e_=e_tile, tt_=tt: e.dma_start(
                        out=out_fm[e_ * 128:(e_ + 1) * 128,
                                   tt_ * 512:(tt_ + 1) * 512],
                        in_=ob[pb_][:]), wait=((ev2, v2),),
                        inc=(wst + str(tile % 2),), dma=True)
                    tile += 1
            gidx += 1

    proj_qk(wq, bqc, bqw, q_fm)
    proj_qk(wk, bkc, bkw, k_fm)

    vt = [ctx.sbuf(f"vt{i}", [128, 512], BF16) for i in range(2)]
    for grp in range(4):
        bi = gidx % 2
        if gidx >= 2:
            ctx.op("sync", lambda e: e.nop(), wait=((mm, 32 * (gidx - 1)),))
        wld, wldv = mk_wK_loader(wv, NCH)(ctx, grp, wbuf[bi], f"g{gidx}")
        for tt in range(T // 128):
            pb = tile % 2
            tw = [(wld, wldv), ("bve", 4)]
            if tile >= 2:
                tw.append((ev2, tile - 1))
            for c in range(NCH):
                r = ctx.op("tensor", lambda e, c_=c, pb_=pb, bi_=bi, tt_=tt: e.matmul(
                    ps[pb_][:], u_res[c_][:, tt_ * 128:(tt_ + 1) * 128],
                    wbuf[bi_][c_][:],
                    start=(c_ == 0), stop=(c_ == NCH - 1)),
                    wait=(tw if c == 0 else ()),
                    inc=((mm,) if c == NCH - 1 else ()))
            mmv = r[mm]
            stw = [(mm, mmv)]
            if tile >= 2:
                stw.append((wst + str(tile % 2), 16 * (tile // 2)))
            v2 = ctx.op("vector", lambda e, pb_=pb, g_=grp: e.tensor_tensor(
                vt[pb_][:], ps[pb_][:], bvbc[:, g_ * 512:(g_ + 1) * 512], ALU.add),
                wait=stw, inc=(ev2,))[ev2]
            ctx.op("gpsimd", lambda e, pb_=pb, g_=grp, tt_=tt: e.dma_start(
                out=v_tm[tt_ * 128:(tt_ + 1) * 128, g_ * 512:(g_ + 1) * 512],
                in_=vt[pb_][:]), wait=((ev2, v2),),
                inc=(wst + str(tile % 2),), dma=True)
            tile += 1
        gidx += 1
    ctx.op("sync", lambda e: e.nop(), wait=(
        (wst + "0", ctx.val(wst + "0")), (wst + "1", ctx.val(wst + "1"))))


def launch_l1a(u2_list, wq1b, wk1b, wv1b, inputs):
    nc = bass.Bass("TRN2", target_bir_lowering=False)
    ctx = Ctx(nc)
    io = {
        "u2_tm": dram_in(nc, "u2_tm", [T, D], BF16),
        "wq1b": dram_in(nc, "wq1b", [D, D], BF16),
        "wk1b": dram_in(nc, "wk1b", [D, D], BF16),
        "wv1b": dram_in(nc, "wv1b", [D, D], BF16),
        "bq": dram_in(nc, "bq", [D]),
        "bk": dram_in(nc, "bk", [D]),
        "bv": dram_in(nc, "bv", [1, D]),
        "q_fm": dram_out(nc, "q_fm", [D, T], BF16),
        "k_fm": dram_out(nc, "k_fm", [D, T], BF16),
        "v_tm": dram_out(nc, "v_tm", [T, D], BF16),
    }
    build_l1_proj(ctx, io)
    ctx.build()
    maps = [{"u2_tm": u2_list[c], "wq1b": wq1b, "wk1b": wk1b, "wv1b": wv1b,
             "bq": inputs["L1_attn_bq"], "bk": inputs["L1_attn_bk"],
             "bv": inputs["L1_attn_bv"].reshape(1, D)} for c in range(NCORES)]
    res = run_bass_kernel_spmd(nc, maps, core_ids=list(range(NCORES)))
    return res.results


def build_l1_attn(ctx, io):
    q_fm, k_fm, v_tm, out_fm = io["q_fm"], io["k_fm"], io["v_tm"], io["attnout1_fm"]
    KAPPA = float(1.0 / np.sqrt(np.float32(HD)))
    NBH = io.get("nbh", 2 * H)
    NI = S // 128

    ia = ctx.sbuf("ia", [128, 128], I32)
    ib = ctx.sbuf("ib", [128, 128], I32)
    maskA = ctx.sbuf("maskA", [128, 128], F32)
    maskK = ctx.sbuf("maskK", [128, 128], F32)
    ones_c = ctx.sbuf("ones_c", [128, 1], BF16)
    ones_r = ctx.sbuf("ones_r", [1, 128], BF16)
    eye_b = ctx.sbuf("eye_b", [128, 128], BF16)
    f32t = ctx.sbuf("f32t", [128, 128], F32)
    ctx.op("gpsimd", lambda e: e.iota(ia[:], [[0, 128]], channel_multiplier=1))
    ctx.op("gpsimd", lambda e: e.iota(ib[:], [[1, 128]], channel_multiplier=0),
           inc=("cst",))
    ctx.op("vector", lambda e: e.tensor_tensor(f32t[:], ib[:], ia[:], ALU.is_gt),
           wait=(("cst", 1),))
    ctx.drain("vector")
    ctx.op("vector", lambda e: e.tensor_scalar(maskA[:], f32t[:], -1e30, None,
                                               ALU.mult))
    ctx.drain("vector")
    ctx.op("vector", lambda e: e.tensor_tensor(maskK[:], ib[:], ia[:], ALU.is_ge))
    ctx.drain("vector")
    ctx.op("vector", lambda e: e.tensor_tensor(f32t[:], ia[:], ib[:], ALU.is_equal))
    ctx.drain("vector")
    ctx.op("vector", lambda e: e.tensor_copy(eye_b[:], f32t[:]))
    ctx.drain("vector")
    ctx.op("vector", lambda e: e.memset(ones_c[:], 1.0))
    ctx.op("vector", lambda e: e.memset(ones_r[:], 1.0), inc=("cst",))
    ctx.drain("vector")
    CST = ctx.val("cst")

    qb = [ctx.sbuf(f"qb{i}", [128, S], BF16) for i in range(2)]
    kb = [ctx.sbuf(f"kb{i}", [128, S], BF16) for i in range(2)]
    vbb = [ctx.sbuf(f"vb{i}", [128, NI * 128], BF16) for i in range(2)]
    mc = ctx.sbuf("mc", [128, 1], F32)
    mt = ctx.sbuf("mt", [128, 1], F32)
    nmc = ctx.sbuf("nmc", [128, NI], BF16)
    nmrow = ctx.sbuf("nmrow", [1, S], BF16)
    ptile = [ctx.sbuf(f"pt{i}", [128, 512], BF16) for i in range(2)]
    lrow = ctx.sbuf("lrow", [1, 512], F32)
    lrb = ctx.sbuf("lrb", [1, 512], BF16)
    lbsb = ctx.sbuf("lbsb", [128, 512], F32)
    ob = [ctx.sbuf(f"ob{i}", [128, 512], BF16) for i in range(2)]
    psA = ctx.psum("psA", [128, 512], F32)
    psP = [ctx.psum(f"psP{i}", [128, 512], F32) for i in range(2)]
    psL = ctx.psum("psL", [1, 512], F32)
    psO = ctx.psum("psO", [128, 512], F32)
    psB2 = ctx.psum("psB2", [128, 512], F32)
    psT = ctx.psum("psT", [1, 128], BF16)

    for bh in range(NBH):
        b_, h_ = divmod(bh, H)
        li = bh % 2
        lw = [("pom", ctx.val("pom")), ("obv", ctx.val("obv"))] if bh >= 2 else \
             [("cst", CST)]
        lv = ctx.op("sync", lambda e, li_=li, b2=b_, hh=h_: e.dma_start(
            out=qb[li_][:], in_=q_fm[hh * 128:(hh + 1) * 128, b2 * S:(b2 + 1) * S]),
            wait=lw, inc=("ql" + str(li),), dma=True)["ql" + str(li)]
        ctx.op("sync", lambda e, li_=li, b2=b_, hh=h_: e.dma_start(
            out=kb[li_][:], in_=k_fm[hh * 128:(hh + 1) * 128, b2 * S:(b2 + 1) * S]),
            inc=("ql" + str(li),), dma=True)
        ctx.op("sync", lambda e, li_=li, b2=b_, hh=h_: e.dma_start(
            out=vbb[li_].ap().rearrange("p (c e) -> p c e", e=128),
            in_=v_tm[b2 * S:(b2 + 1) * S, hh * 128:(hh + 1) * 128]
                .rearrange("(c p) e -> p c e", p=128)),
            inc=("ql" + str(li),), dma=True)
        LV = ctx.val("ql" + str(li))

        for it in range(NI):
            wtot = (it + 1) * 128
            segs = []
            j0 = 0
            while j0 < wtot:
                w = min(512, wtot - j0)
                segs.append((j0, w))
                j0 += w
            for si, (j0, w) in enumerate(segs):
                mw = [("mst", ctx.val("mst"))]
                if it == 0 and si == 0:
                    mw = [("ql" + str(li), LV)]
                ctx.op("tensor", lambda e, li_=li, it_=it, j_=j0, w_=w: e.matmul(
                    psA[:, 0:w_], qb[li_][:, it_ * 128:(it_ + 1) * 128],
                    kb[li_][:, j_:j_ + w_], start=True, stop=True),
                    wait=mw, inc=("mmA",))
                if si == len(segs) - 1:
                    ctx.op("vector", lambda e, w_=w: e.tensor_tensor(
                        psA[:, w_ - 128:w_], psA[:, w_ - 128:w_], maskA[:], ALU.add),
                        wait=(("mmA", ctx.val("mmA")),))
                    ctx.drain("vector")
                    wv_ = ()
                else:
                    wv_ = [("mmA", ctx.val("mmA"))]
                ctx.op("vector", lambda e, w_=w: e.reduce_max(
                    mt[:], psA[:, 0:w_], axis=AX), wait=wv_)
                ctx.drain("vector")
                if si == 0:
                    ctx.op("vector", lambda e: e.tensor_copy(mc[:], mt[:]),
                           inc=("mst",))
                else:
                    ctx.op("vector", lambda e: e.tensor_tensor(
                        mc[:], mc[:], mt[:], ALU.max), inc=("mst",))
                ctx.drain("vector")
            ctx.op("vector", lambda e, it_=it: e.tensor_scalar(
                nmc[:, it_:it_ + 1], mc[:], -1.0, None, ALU.mult), inc=("nmd",))
            ctx.drain("vector")
        for it in range(NI):
            ctx.op("tensor", lambda e, it_=it: e.transpose(
                psT[:], nmc.ap()[:, it_:it_ + 1], eye_b[:]),
                wait=(("nmd", ctx.val("nmd")), ("nrv", ctx.val("nrv"))),
                inc=("nrt",))
            ctx.op("vector", lambda e, it_=it: e.tensor_copy(
                nmrow[:, it_ * 128:(it_ + 1) * 128], psT[:]),
                wait=(("nrt", ctx.val("nrt")),), inc=("nrv",))
            ctx.drain("vector")
        NRV = ctx.val("nrv")

        for ib2 in range(4):
            njt = (ib2 + 1) * 4
            POM_PREV = ctx.val("pom")
            pomv = []
            for jt in range(njt):
                pb = jt % 2
                if ib2 == 0 and jt == 0:
                    rw = [("nrv", NRV)]
                elif jt == 0:
                    rw = [("obv", ctx.val("obv"))]
                elif jt >= 2:
                    rw = [("pex", ctx.val("pex"))]
                else:
                    rw = []
                ctx.op("tensor", lambda e, li_=li, pb_=pb, i2=ib2: e.matmul(
                    psP[pb_][:], ones_r[:], nmrow[:, i2 * 512:(i2 + 1) * 512],
                    start=True, stop=False), wait=rw)
                ctx.op("tensor", lambda e, li_=li, pb_=pb, i2=ib2, jt_=jt: e.matmul(
                    psP[pb_][:], kb[li_][:, jt_ * 128:(jt_ + 1) * 128],
                    qb[li_][:, i2 * 512:(i2 + 1) * 512],
                    start=False, stop=True), inc=("mmP",))
                pw = [("mmP", ctx.val("mmP")), ("pom", ctx.val("pom"))]
                if jt >= 2:
                    pw.append(("pmk", ctx.val("pmk")))
                ctx.op("scalar", lambda e, pb_=pb: e.activation(
                    ptile[pb_][:], psP[pb_][:], AF.Exp, scale=KAPPA),
                    wait=pw, inc=("pex",))
                jtd = jt - ib2 * 4
                mkw = [("pex", ctx.val("pex"))]
                if jtd >= 0:
                    if jtd > 0:
                        ctx.op("vector", lambda e, pb_=pb, d_=jtd: e.memset(
                            ptile[pb_][:, 0:d_ * 128], 0.0), wait=mkw)
                        ctx.drain("vector")
                        mkw = ()
                    ctx.op("vector", lambda e, pb_=pb, d_=jtd: e.tensor_tensor(
                        ptile[pb_][:, d_ * 128:(d_ + 1) * 128],
                        ptile[pb_][:, d_ * 128:(d_ + 1) * 128], maskK[:], ALU.mult),
                        wait=mkw)
                    ctx.drain("vector")
                ctx.op("vector", lambda e: e.nop(),
                       wait=(("pex", ctx.val("pex")),), inc=("pmk",))
                PMK = ctx.val("pmk")
                ctx.op("tensor", lambda e, pb_=pb, jt_=jt, nj=njt: e.matmul(
                    psL[:], ones_c[:], ptile[pb_][:],
                    start=(jt_ == 0), stop=(jt_ == nj - 1)),
                    wait=(("pmk", PMK),))
                pomv.append(ctx.op("tensor",
                    lambda e, pb_=pb, li_=li, jt_=jt, nj=njt: e.matmul(
                    psO[:], vbb[li_][:, jt_ * 128:(jt_ + 1) * 128], ptile[pb_][:],
                    start=(jt_ == 0), stop=(jt_ == nj - 1)),
                    inc=("pom",))["pom"])
            PVM = ctx.val("pom")
            ctx.op("vector", lambda e: e.tensor_copy(lrow[:], psL[:]),
                   wait=(("pom", PVM),))
            ctx.drain("vector")
            ctx.op("vector", lambda e: e.reciprocal(lrow[:], lrow[:]))
            ctx.drain("vector")
            ctx.op("vector", lambda e: e.tensor_copy(lrb[:], lrow[:]), inc=("lrv",))
            ctx.drain("vector")
            ctx.op("tensor", lambda e: e.matmul(
                psB2[:], ones_r[:], lrb[:], start=True, stop=True),
                wait=(("lrv", ctx.val("lrv")),), inc=("lbm",))
            ctx.op("vector", lambda e: e.tensor_copy(lbsb[:], psB2[:]),
                   wait=(("lbm", ctx.val("lbm")),))
            ctx.drain("vector")
            ow = []
            oi = bh * 4 + ib2
            if oi >= 2:
                ow.append(("ost" + str(oi % 2), 16 * (oi // 2)))
            ov = ctx.op("vector", lambda e, oi_=oi: e.tensor_tensor(
                ob[oi_ % 2][:], psO[:], lbsb[:], ALU.mult), wait=ow,
                inc=("obv",))["obv"]
            ctx.op("gpsimd", lambda e, b2=b_, hh=h_, i2=ib2, oi_=oi: e.dma_start(
                out=out_fm[hh * 128:(hh + 1) * 128,
                           b2 * S + i2 * 512: b2 * S + (i2 + 1) * 512],
                in_=ob[oi_ % 2][:]),
                wait=(("obv", ov),), inc=("ost" + str(oi % 2),), dma=True)
    ctx.op("sync", lambda e: e.nop(), wait=(
        ("ost0", ctx.val("ost0")), ("ost1", ctx.val("ost1"))))


def launch_l1b(q_list, k_list, v_list, nbh=2 * H):
    nc = bass.Bass("TRN2", target_bir_lowering=False)
    ctx = Ctx(nc)
    io = {
        "q_fm": dram_in(nc, "q_fm", [D, T], BF16),
        "k_fm": dram_in(nc, "k_fm", [D, T], BF16),
        "v_tm": dram_in(nc, "v_tm", [T, D], BF16),
        "attnout1_fm": dram_out(nc, "attnout1_fm", [D, T], BF16),
        "nbh": nbh,
    }
    build_l1_attn(ctx, io)
    ctx.build()
    maps = [{"q_fm": q_list[c], "k_fm": k_list[c], "v_tm": v_list[c]}
            for c in range(NCORES)]
    res = run_bass_kernel_spmd(nc, maps, core_ids=list(range(NCORES)))
    return res.results


def emit_dequant(ctx, p, packed, scales, out_b, n_rows, n_ints, eye_f, eyew,
                 blocks_per_iter=2):
    NBLK = n_rows // 128
    NIT = NBLK // blocks_per_iter
    W = n_ints * blocks_per_iter
    NG = W // 16
    n_grp_rows = (n_ints * 8) // 128
    pk = [ctx.sbuf(f"{p}pk{i}", [128, W], I32) for i in range(2)]
    ob = [ctx.sbuf(f"{p}ob{i}", [128, 8 * W], BF16) for i in range(2)]
    sc16 = [ctx.sbuf(f"{p}s16{i}", [n_grp_rows, 128 * blocks_per_iter], F16)
            for i in range(2)]
    scf = ctx.sbuf(f"{p}scf", [n_grp_rows, 128 * blocks_per_iter], F32)
    scT = ctx.sbuf(f"{p}scT", [128, blocks_per_iter * n_grp_rows], F32)
    scex = ctx.sbuf(f"{p}sx", [128, W], F32)
    nn = ctx.sbuf(f"{p}nn", [128, W], I32)
    cc = ctx.sbuf(f"{p}cc", [128, W], F32)
    ss = ctx.sbuf(f"{p}ss", [128, W], F32)
    c8 = ctx.sbuf(f"{p}c8", [128, W], F32)
    rA = ctx.sbuf(f"{p}rA", [128, W], F32)
    rB = ctx.sbuf(f"{p}rB", [128, W], F32)
    tt_ = ctx.sbuf(f"{p}tt", [128, W], F32)
    cm4 = ctx.sbuf(f"{p}c4", [128, 1], F32)
    cm12 = ctx.sbuf(f"{p}c12", [128, 1], F32)
    pst = ctx.psum(f"{p}ps", [128, 128], F32)
    ctx.op("vector", lambda e: e.memset(cm4[:], -4.0))
    ctx.op("vector", lambda e: e.memset(cm12[:], -12.0), inc=(p + "cm",))
    ctx.drain("vector")
    for it in range(NIT):
        bi = it % 2
        r0 = it * blocks_per_iter * 128
        lw = [(p + "od", 8 * it)] if it >= 2 else []
        lv = ctx.op("sync", lambda e, bi_=bi, r_=r0: e.dma_start(
            out=pk[bi_][:].rearrange("p (b f) -> p b f", b=blocks_per_iter),
            in_=packed[r_:r_ + blocks_per_iter * 128, :]
                .rearrange("(b p) f -> p b f", p=128)),
            wait=lw, inc=(p + "pl" + str(bi),), dma=True)[p + "pl" + str(bi)]
        ctx.op("sync", lambda e, bi_=bi, r_=r0: e.dma_start(
            out=sc16[bi_][:], in_=scales[:, r_:r_ + blocks_per_iter * 128]),
            inc=(p + "pl" + str(bi),), dma=True)
        LV = ctx.val(p + "pl" + str(bi))
        sw = [(p + "pl" + str(bi), LV)]
        if it >= 1:
            sw.append((p + "su", it))
        ctx.op("vector", lambda e, bi_=bi: e.tensor_copy(scf[:], sc16[bi_][:]),
               wait=sw, inc=(p + "sf",))
        ctx.drain("vector")
        SF = ctx.val(p + "sf")
        for b2 in range(blocks_per_iter):
            ctx.op("tensor", lambda e, b_=b2: e.transpose(
                pst[:, 0:n_grp_rows], scf.ap()[:, b_ * 128:(b_ + 1) * 128],
                eye_f[0:n_grp_rows, 0:n_grp_rows]),
                wait=[(p + "sf", SF), eyew] + (
                    [(p + "se", ctx.val(p + "se"))] if ctx.val(p + "se") else []),
                inc=(p + "st",))
            ctx.op("vector", lambda e, b_=b2: e.tensor_copy(
                scT[:, b_ * n_grp_rows:(b_ + 1) * n_grp_rows], pst[:, 0:n_grp_rows]),
                wait=((p + "st", ctx.val(p + "st")),), inc=(p + "se",))
            ctx.drain("vector")
        for r in range(16):
            ctx.op("vector", lambda e, r_=r: e.tensor_copy(
                scex.ap().rearrange("p (g r) -> p g r", r=16)[:, :, r_], scT[:]))
        ctx.drain("vector")
        ctx.op("vector", lambda e: e.nop(), inc=(p + "sx",))
        SX = ctx.val(p + "sx")
        for k in range(8):
            ctx.op("vector", lambda e, bi_=bi, k_=k: e.tensor_scalar(
                nn[:], pk[bi_][:], 4 * k_, 15, ALU.logical_shift_right,
                ALU.bitwise_and), wait=((p + "sx", SX),) if k == 0 else ())
            ctx.drain("vector")
            ctx.op("vector", lambda e: e.tensor_copy(cc[:], nn[:]))
            ctx.drain("vector")
            ctx.op("vector", lambda e: e.tensor_scalar(ss[:], cc[:], 8.0, None,
                                                       ALU.is_ge))
            ctx.drain("vector")
            ctx.op("vector", lambda e: e.scalar_tensor_tensor(
                c8[:], ss[:], -8.0, cc[:], ALU.mult, ALU.add), inc=(p + "v1",))
            ctx.op("scalar", lambda e: e.activation(rA[:], c8[:], AF.Relu,
                                                    bias=cm4[:]),
                   wait=((p + "v1", ctx.val(p + "v1")), (p + "cm", 1)))
            ctx.op("scalar", lambda e: e.activation(rB[:], c8[:], AF.Relu,
                                                    bias=cm12[:], scale=2.0),
                   inc=(p + "a1",))
            ctx.op("vector", lambda e: e.tensor_tensor(tt_[:], c8[:], rA[:], ALU.add),
                   wait=((p + "a1", ctx.val(p + "a1")),))
            ctx.drain("vector")
            ctx.op("vector", lambda e: e.tensor_tensor(tt_[:], tt_[:], rB[:], ALU.add))
            ctx.drain("vector")
            ctx.op("vector", lambda e: e.tensor_scalar(ss[:], ss[:], -1.0, 0.5,
                                                       ALU.mult, ALU.add))
            ctx.drain("vector")
            ctx.op("vector", lambda e: e.tensor_tensor(tt_[:], tt_[:], ss[:],
                                                       ALU.mult))
            ctx.drain("vector")
            ow = []
            if it >= 2 and k == 0:
                ow.append((p + "os" + str(bi), 16 * (it // 2)))
            ctx.op("vector", lambda e, bi_=bi, k_=k: e.scalar_tensor_tensor(
                ob[bi_].ap().rearrange("p (i k) -> p i k", k=8)[:, :, k_],
                tt_[:], 1.0, scex[:], ALU.mult, ALU.mult), wait=ow, inc=(p + "od",))
            ctx.drain("vector")
        ctx.op("vector", lambda e: e.nop(), inc=(p + "su",))
        OD = ctx.val(p + "od")
        ctx.op("gpsimd", lambda e, bi_=bi, r_=r0: e.dma_start(
            out=out_b[r_:r_ + blocks_per_iter * 128, :]
                .rearrange("(b p) h -> p b h", p=128),
            in_=ob[bi_][:].rearrange("p (b h) -> p b h", b=blocks_per_iter)),
            wait=((p + "od", OD),), inc=(p + "os" + str(bi),), dma=True)
    return (p + "os0", ctx.val(p + "os0")), (p + "os1", ctx.val(p + "os1"))


def launch_stage_q(inputs):
    nc = bass.Bass("TRN2", target_bir_lowering=False)
    ctx = Ctx(nc)
    gup = dram_in(nc, "gup", [2 * I1, D // 8], I32)
    gus = dram_in(nc, "gus", [D // 128, 2 * I1], F16)
    dnp = dram_in(nc, "dnp", [D, I1 // 8], I32)
    dns = dram_in(nc, "dns", [I1 // 128, D], F16)
    Wgu_b = dram_out(nc, "Wgu_b", [2 * I1, D], BF16)
    Wdn_b = dram_out(nc, "Wdn_b", [D, I1], BF16)
    eyf, eyb, eyw = emit_eye(ctx, "ey")
    w1 = emit_dequant(ctx, "dg", gup, gus, Wgu_b, 2 * I1, D // 8, eyf, eyw)
    w2 = emit_dequant(ctx, "dd", dnp, dns, Wdn_b, D, I1 // 8, eyf, eyw)
    ctx.op("sync", lambda e: e.nop(), wait=tuple(list(w1) + list(w2)))
    ctx.build()
    m = {"gup": inputs["L1_ffn_gate_up_packed"], "gus": inputs["L1_ffn_gate_up_scales"],
         "dnp": inputs["L1_ffn_down_packed"], "dns": inputs["L1_ffn_down_scales"]}
    res = run_bass_kernel_spmd(nc, [m] * NCORES, core_ids=list(range(NCORES)))
    return res.results


def mk_wT_loader_off(w_hbm, row_off, KCH=NCH):
    def load(ctx, g, wbuf, sem):
        for c in range(KCH):
            ctx.op("sync", lambda e, c_=c, g_=g: e.dma_start_transpose(
                wbuf[c_][:],
                w_hbm[row_off + g_ * 512:row_off + (g_ + 1) * 512,
                      c_ * 128:(c_ + 1) * 128]),
                inc=(sem,), dma=True)
        return sem, ctx.val(sem)
    return load


def launch_l1c(att_list, wo1b, x2_list, inputs):
    nc = bass.Bass("TRN2", target_bir_lowering=False)
    ctx = Ctx(nc)
    att_fm = dram_in(nc, "att_fm", [D, T], BF16)
    wo1b_t = dram_in(nc, "wo1b", [D, D], BF16)
    bo = dram_in(nc, "bo", [1, D])
    x2 = dram_in(nc, "x2", [T, D])
    x3 = dram_out(nc, "x3", [T, D])
    u3 = dram_out(nc, "u3_tm", [T, D], BF16)
    y_tm = nc.dram_tensor("y_tm", [T, D], F32, kind="Internal").ap()
    onesr = ctx.sbuf("onesr", [1, 128], BF16)
    bobc = ctx.sbuf("bobc", [128, D], F32)
    bes = ExitStack()
    bor = bes.enter_context(nc.sbuf_tensor("bor", [1, D], F32))
    borb = bes.enter_context(nc.sbuf_tensor("borb", [1, D], BF16))
    psb = ctx.psum("psb", [128, 512], F32)
    ctx.op("vector", lambda e: e.memset(onesr[:], 1.0))
    ctx.drain("vector")
    lv = ctx.op("sync", lambda e: e.dma_start(out=bor[:], in_=bo),
                inc=("bol",), dma=True)["bol"]
    ctx.op("vector", lambda e: e.tensor_copy(borb[:], bor[:]), wait=(("bol", lv),),
           inc=("bob",))
    ctx.drain("vector")
    for qq in range(4):
        ctx.op("tensor", lambda e, q_=qq: e.matmul(
            psb[:], onesr[:], borb[:, q_ * 512:(q_ + 1) * 512], start=True, stop=True),
            wait=(("bob", 1),) if qq == 0 else (("boe", qq),), inc=("bom",))
        ctx.op("vector", lambda e, q_=qq: e.tensor_copy(
            bobc[:, q_ * 512:(q_ + 1) * 512], psb[:]),
            wait=(("bom", qq + 1),), inc=("boe",))
        ctx.drain("vector")
    bes.close()
    sw = emit_out_proj(ctx, {}, "op", att_fm, mk_wK_loader(wo1b_t, NCH), y_tm, D,
                       bias_bc=bobc, pre_wait=[("boe", 4)])
    ew = emit_epilogue(ctx, y_tm, x2, x3, u3, "ep", pre_wait=sw)
    ctx.op("sync", lambda e: e.nop(), wait=ew)
    ctx.build()
    maps = [{"att_fm": att_list[c], "wo1b": wo1b, "x2": x2_list[c],
             "bo": inputs["L1_attn_bo"].reshape(1, D)} for c in range(NCORES)]
    res = run_bass_kernel_spmd(nc, maps, core_ids=list(range(NCORES)))
    return res.results


def launch_l1d(u3_list, Wgu_b):
    nc = bass.Bass("TRN2", target_bir_lowering=False)
    ctx = Ctx(nc)
    u3_tm = dram_in(nc, "u3_tm", [T, D], BF16)
    Wgu = dram_in(nc, "Wgu", [2 * I1, D], BF16)
    h_fm = dram_out(nc, "h_fm", [I1, T], BF16)
    u_res = [ctx.sbuf(f"u{c}", [128, T], BF16, is_transpose_dest=True)
             for c in range(NCH)]
    uld, uldv = emit_load_u_fm(ctx, u3_tm, u_res, "ul")
    sw = emit_glu_proj(ctx, "gl", u_res, (uld, uldv),
                       mk_wT_loader_off(Wgu, 0), mk_wT_loader_off(Wgu, I1),
                       I1 // 128, h_fm)
    ctx.op("sync", lambda e: e.nop(), wait=sw)
    ctx.build()
    maps = [{"u3_tm": u3_list[c], "Wgu": Wgu_b} for c in range(NCORES)]
    res = run_bass_kernel_spmd(nc, maps, core_ids=list(range(NCORES)))
    return res.results


def launch_l1e(h_list, Wdn_b, x3_list):
    nc = bass.Bass("TRN2", target_bir_lowering=False)
    ctx = Ctx(nc)
    h_fm = dram_in(nc, "h_fm", [I1, T], BF16)
    Wdn = dram_in(nc, "Wdn", [D, I1], BF16)
    x3 = dram_in(nc, "x3", [T, D])
    x4 = dram_out(nc, "x4", [T, D])
    y_tm = nc.dram_tensor("y_tm", [T, D], F32, kind="Internal").ap()
    u4 = nc.dram_tensor("u4_tm", [T, D], BF16, kind="Internal").ap()
    sw = emit_out_proj(ctx, {}, "op", h_fm, mk_wT_loader(Wdn, I1 // 128), y_tm, I1)
    ew = emit_epilogue(ctx, y_tm, x3, x4, u4, "ep", rms=True, pre_wait=sw)
    ctx.op("sync", lambda e: e.nop(), wait=ew)
    ctx.build()
    maps = [{"h_fm": h_list[c], "Wdn": Wdn_b, "x3": x3_list[c]}
            for c in range(NCORES)]
    res = run_bass_kernel_spmd(nc, maps, core_ids=list(range(NCORES)))
    return res.results
